# version 10
# baseline (speedup 1.0000x reference)
import math
import numpy as np
import ml_dtypes
import concourse.bass as bass
import concourse.mybir as mybir
from concourse.bass_utils import run_bass_kernel_spmd

F32 = mybir.dt.float32
BF16 = mybir.dt.bfloat16
AF = mybir.ActivationFunctionType
ALU = mybir.AluOpType
AX = mybir.AxisListType

T = 4096
D = 1024
H = 8
DV = 256
NE = 16
FE = 512
FS = 1024
SCALE = 1.0 / math.sqrt(128.0)
LAMBDA_INIT = 0.8 - 0.6 * math.exp(-0.3)
EPS = 1e-5

ENGINES = ("tensor", "scalar", "vector", "gpsimd", "sync")


DMA_POOL = 8


class Op:
    __slots__ = ("eng", "fn", "waits", "dma", "marked", "sem", "val", "pre", "tag")

    def __init__(self, eng, fn, waits, dma, tag):
        self.eng, self.fn, self.dma, self.tag = eng, fn, dma, tag
        self.waits = [w for w in waits if w is not None]
        self.marked = False
        self.sem = None
        self.val = None
        self.pre = None


class Prog:
    def __init__(self):
        self.ops = []

    def add(self, eng, fn, waits=(), dma=None, tag=""):
        op = Op(eng, fn, waits, dma, tag)
        self.ops.append(op)
        return op

    def resolve(self, nc, ctx):
        groups = sorted({op.dma for op in self.ops if op.dma is not None})
        sems = {}
        for e in ENGINES:
            sems["e_" + e] = ctx.enter_context(nc.semaphore("e_" + e))
        pool = {}
        for g in groups:
            pool[g] = [ctx.enter_context(nc.semaphore(f"d_{g}{i}")) for i in range(DMA_POOL)]
        CHAIN = ("vector", "scalar")
        for op in self.ops:
            kept = []
            for w in op.waits:
                if w.dma is None and w.eng == op.eng and op.eng not in CHAIN:
                    continue
                if w.dma is None and w.eng == op.eng and op.eng in CHAIN:
                    continue
                w.marked = True
                kept.append(w)
            op.waits = kept
        for op in self.ops:
            if op.eng in CHAIN and op.dma is None:
                op.marked = True
        counts = {("e_" + e): 0 for e in ENGINES}
        rr = {g: 0 for g in groups}
        dma_counts = {}
        for op in self.ops:
            if op.dma is not None:
                op.marked = True
                i = rr[op.dma]
                rr[op.dma] = (i + 1) % DMA_POOL
                sem = pool[op.dma][i]
                prevv = dma_counts.get(id(sem), 0)
                if prevv:
                    op.pre = (sem, prevv)
                op.sem = sem
                op.val = prevv + 16
                dma_counts[id(sem)] = op.val
            elif op.marked:
                cnt = counts["e_" + op.eng]
                if op.eng in CHAIN and cnt > 0:
                    op.pre = (sems["e_" + op.eng], cnt)
                counts["e_" + op.eng] = cnt + 1
                op.sem = sems["e_" + op.eng]
                op.val = cnt + 1
            elif op.eng in CHAIN:
                pass
        self.sems = sems
        self.drain_waits = []
        seen = set()
        for op in self.ops:
            if op.dma is not None and id(op.sem) not in seen:
                seen.add(id(op.sem))
        finals = {}
        for op in self.ops:
            if op.dma is not None:
                finals[id(op.sem)] = (op.sem, op.val)
        self.drain_waits = list(finals.values())

    def emit(self, eng_name, eng):
        watermark = {}
        for op in self.ops:
            if op.eng != eng_name:
                continue
            pend = list(op.waits)
            for w in pend:
                key = id(w.sem)
                if watermark.get(key, -1) >= w.val:
                    continue
                eng.wait_ge(w.sem, w.val)
                watermark[key] = w.val
            if op.pre is not None:
                sem, v = op.pre
                if watermark.get(id(sem), -1) < v:
                    eng.wait_ge(sem, v)
                    watermark[id(sem)] = v
            inst = op.fn(eng)
            if op.marked:
                inst.then_inc(op.sem, 16 if op.dma is not None else 1)
        if eng_name == "sync":
            for sem, v in self.drain_waits:
                if watermark.get(id(sem), -1) < v:
                    eng.wait_ge(sem, v)


def build_program(dbg=()):
    nc = bass.Bass("TRN2", target_bir_lowering=False)
    dt = nc.dram_tensor
    E = {}

    for n, s in (("XT", [D, T]), ("WqT", [128, 16384]), ("WkT", [128, 16384]),
                 ("WvT", [128, 16384]), ("WoT", [128, 16384]),
                 ("lq1", [H, 128]), ("lk1", [H, 128]), ("lq2", [H, 128]), ("lk2", [H, 128]),
                 ("rms_scale", [1, DV]), ("rw", [128, 128]),
                 ("gw", [NE, 128, 4096]), ("uw", [NE, 128, 4096]), ("dw", [NE, 128, 4096]),
                 ("sgw", [128, 8192]), ("suw", [128, 8192]), ("sdw", [128, 8192]),
                 ("identf", [128, 128]), ("onesf", [1, 128])):
        E[n] = dt(n, s, F32, kind="ExternalInput").ap()
    E["self"] = dt("self", [NE, NE * 128], F32, kind="ExternalInput").ap()
    E["YT"] = dt("YT", [D, T], F32, kind="ExternalOutput").ap()

    def scratch(name, shape, dtype):
        kind = "ExternalOutput" if name in dbg else "Internal"
        return dt(name, shape, dtype, kind=kind).ap()

    if "probe_u0" in dbg:
        E["DBG_P"] = dt("DBG_P", [128, 4096], BF16, kind="ExternalOutput").ap()
        E["DBG_E2"] = dt("DBG_E2", [128, 4096], BF16, kind="ExternalOutput").ap()
        E["DBG_DEN"] = dt("DBG_DEN", [128, 32], F32, kind="ExternalOutput").ap()
        E["DBG_PT"] = dt("DBG_PT", [128, 4096], BF16, kind="ExternalOutput").ap()
        E["DBG_ON"] = dt("DBG_ON", [128, 256], F32, kind="ExternalOutput").ap()
        E["DBG_OTM"] = dt("DBG_OTM", [128, 256], BF16, kind="ExternalOutput").ap()
        E["DBG_Q"] = dt("DBG_Q", [128, 8192], BF16, kind="ExternalOutput").ap()
        E["DBG_K"] = dt("DBG_K", [128, 8192], BF16, kind="ExternalOutput").ap()
        E["DBG_V"] = dt("DBG_V", [128, 8192], BF16, kind="ExternalOutput").ap()
    E["QTh"] = scratch("QTh", [2048, T], BF16)
    E["KTh"] = scratch("KTh", [2048, T], BF16)
    E["VH"] = scratch("VH", [T, 2048], BF16)
    E["OT"] = scratch("OT", [2048, T], BF16)
    E["X1F"] = scratch("X1F", [D, T], F32)
    E["X1B"] = scratch("X1B", [D, T], BF16)

    from contextlib import ExitStack
    with ExitStack() as ctx:
        E["AB"] = ctx.enter_context(nc.sbuf_tensor("AB", [128, 73728], BF16))
        E["AFr"] = ctx.enter_context(nc.sbuf_tensor("AFr", [128, 15488], F32))
        E["S1p"] = ctx.enter_context(nc.psum_tensor("S1p", [128, 1024], F32))
        E["S2p"] = ctx.enter_context(nc.psum_tensor("S2p", [128, 1024], F32))
        E["trp0"] = ctx.enter_context(nc.psum_tensor("trp0", [128, 1024], BF16))
        E["trp1"] = ctx.enter_context(nc.psum_tensor("trp1", [128, 1024], BF16))
        E["ovp"] = ctx.enter_context(nc.psum_tensor("ovp", [128, 512], F32))
        E["pob"] = ctx.enter_context(nc.psum_tensor("pob", [128, 1024], BF16))

        p = Prog()
        emit_all(p, E, dbg)
        p.resolve(nc, ctx)

        with nc.Block() as block:
            @block.tensor
            def _(e):
                p.emit("tensor", e)

            @block.scalar
            def _(e):
                p.emit("scalar", e)

            @block.vector
            def _(e):
                p.emit("vector", e)

            @block.gpsimd
            def _(e):
                p.emit("gpsimd", e)

            @block.sync
            def _(e):
                p.emit("sync", e)
    return nc


def emit_all(p, E, dbg):
    AB, AFr = E["AB"], E["AFr"]
    S1p, S2p, trp0, trp1, ovp, pob = E["S1p"], E["S2p"], E["trp0"], E["trp1"], E["ovp"], E["pob"]

    def ab(off, n, p0=0, np_=128):
        return AB[p0:p0 + np_, off:off + n]

    def af(off, n, p0=0, np_=128):
        return AFr[p0:p0 + np_, off:off + n]

    fIDENT, fONES, fRMSB, fLAM = 0, 128, 256, 512
    fBIG = 640
    bIDENT, bONES, bSEL = 0, 128, 256
    bBIG = bSEL + 2048

    ld_id = p.add("sync", lambda e: e.dma_start(out=af(fIDENT, 128), in_=E["identf"]), dma="win")
    ld_on = p.add("sync", lambda e: e.dma_start(out=af(fONES, 128, 0, 1), in_=E["onesf"]), dma="win")
    ld_idb = p.add("gpsimd", lambda e: e.dma_start(out=ab(bIDENT, 128), in_=E["identf"]), dma="wing")
    ld_sel = p.add("gpsimd", lambda e: e.dma_start(out=ab(bSEL, 2048, 0, NE), in_=E["self"]), dma="wing")
    ms_ones = p.add("gpsimd", lambda e: e.memset(ab(bONES, 128), 1.0))

    fSM = fBIG
    ld_rs = p.add("sync", lambda e: e.dma_start(out=af(fSM, DV, 0, 1), in_=E["rms_scale"]), dma="win")
    mm_rs = p.add("tensor", lambda e: e.matmul(ovp[:, 0:256], af(fONES, 128, 0, 1), af(fSM, DV, 0, 1), start=True, stop=True), waits=[ld_on, ld_rs])
    ev_rs = p.add("vector", lambda e: e.tensor_scalar(af(fRMSB, DV), ovp[:, 0:256], 1.0 - LAMBDA_INIT, None, ALU.mult), waits=[mm_rs])

    fL = fBIG + 256
    fLs = fL + 512
    ldl = []
    for i, t in enumerate(("lq1", "lk1", "lq2", "lk2")):
        ldl.append(p.add("sync", lambda e, t=t, i=i: e.dma_start(out=af(fL + 128 * i, 128, 0, 8), in_=E[t]), dma="win"))
    p.add("vector", lambda e: e.tensor_tensor(af(fL, 128, 0, 8), af(fL, 128, 0, 8), af(fL + 128, 128, 0, 8), ALU.mult), waits=ldl[:2])
    p.add("vector", lambda e: e.tensor_tensor(af(fL + 256, 128, 0, 8), af(fL + 256, 128, 0, 8), af(fL + 384, 128, 0, 8), ALU.mult), waits=ldl[2:])
    rs1 = p.add("vector", lambda e: e.tensor_reduce(af(fLs, 1, 0, 8), af(fL, 128, 0, 8), AX.X, ALU.add))
    rs2 = p.add("vector", lambda e: e.tensor_reduce(af(fLs + 1, 1, 0, 8), af(fL + 256, 128, 0, 8), AX.X, ALU.add))
    ex12 = p.add("scalar", lambda e: e.activation(af(fLs + 2, 2, 0, 8), af(fLs, 2, 0, 8), AF.Exp), waits=[rs1, rs2])
    p.add("vector", lambda e: e.tensor_tensor(af(fLs + 4, 1, 0, 8), af(fLs + 3, 1, 0, 8), af(fLs + 2, 1, 0, 8), ALU.subtract), waits=[ex12])
    nl2 = p.add("vector", lambda e: e.tensor_scalar(af(fLs + 5, 1, 0, 8), af(fLs + 4, 1, 0, 8), -LAMBDA_INIT, None, ALU.add))
    trl = p.add("tensor", lambda e: e.transpose(ovp[0:1, 256:264], af(fLs + 5, 1, 0, 8), af(fIDENT, 8, 0, 8)), waits=[nl2, ev_rs, ld_id])
    evl = p.add("vector", lambda e: e.tensor_copy(af(fLs + 6, 8, 0, 1), ovp[0:1, 256:264]), waits=[trl])
    mml = p.add("tensor", lambda e: e.matmul(ovp[:, 264:272], af(fONES, 128, 0, 1), af(fLs + 6, 8, 0, 1), start=True, stop=True), waits=[evl])
    nlam = p.add("vector", lambda e: e.tensor_copy(af(fLAM, 8), ovp[:, 264:272]), waits=[mml])

    bXTB = bBIG
    bW = bXTB + 32768
    bW2 = bW + 16384
    bSTG = bW2 + 16384
    assert bSTG + 4096 <= 73728

    xchunks = [p.add("gpsimd",
                     lambda e, c=c: e.dma_start(out=ab(bXTB + 4096 * c, 4096), in_=E["XT"][c * 128:(c + 1) * 128, :]),
                     dma="wing") for c in range(8)]
    ldwq = p.add("gpsimd", lambda e: e.dma_start(out=ab(bW, 16384), in_=E["WqT"]), dma="wing")
    ldwk = p.add("gpsimd", lambda e: e.dma_start(out=ab(bW2, 16384), in_=E["WkT"]), dma="wing")

    vh_r = E["VH"].rearrange("(t p) o -> p t o", p=128)

    def proj_fm(dst, woff, wait_w, tag, stg_prev=None, ev_seed=None):
        dmas = []
        prev_stg = stg_prev
        prev_ev = dict(ev_seed) if ev_seed else {}
        for m in range(16):
            evs = []
            for n in range(8):
                bank = (m * 8 + n) % 2
                psl = (S1p[:, 0:512], S1p[:, 512:1024])[bank]
                mms = []
                for c in range(8):
                    w = []
                    if c == 0:
                        w = [wait_w] + xchunks
                        if prev_ev.get(bank) is not None:
                            w.append(prev_ev[bank])
                    mms.append(p.add(
                        "tensor",
                        lambda e, m=m, n=n, c=c, psl=psl, woff=woff: e.matmul(
                            psl, ab(woff + 2048 * c + 128 * m, 128),
                            ab(bXTB + 4096 * c + 512 * n, 512),
                            start=(c == 0), stop=(c == 7)),
                        waits=w, tag=f"{tag}mm"))
                w2 = [mms[-1]]
                if n == 0:
                    if dmas:
                        w2.append(dmas[-1])
                    elif prev_stg is not None:
                        w2.append(prev_stg)
                ev = p.add("scalar", lambda e, n=n, psl=psl: e.copy(ab(bSTG + 512 * n, 512), psl), waits=w2)
                prev_ev[bank] = ev
                evs.append(ev)
            dmas.append(p.add(
                "sync",
                lambda e, m=m, dst=dst: e.dma_start(out=dst[m * 128:(m + 1) * 128, :], in_=ab(bSTG, 4096)),
                waits=evs, dma="out", tag=f"{tag}dma"))
        return dmas, prev_ev

    qdmas, qev = proj_fm(E["QTh"], bW, ldwq, "q")
    ldwv = p.add("gpsimd", lambda e: e.dma_start(out=ab(bW, 16384), in_=E["WvT"]), waits=[qdmas[-1]], dma="wing")
    kdmas, _ = proj_fm(E["KTh"], bW2, ldwk, "k", stg_prev=qdmas[-1], ev_seed=qev)

    vdmas = []
    prev_ev = {}
    for m in range(32):
        evs = []
        for gi in range(4):
            bank = (m * 4 + gi) % 2
            psl = (S2p[:, 0:512], S2p[:, 512:1024])[bank]
            mms = []
            for c in range(8):
                w = []
                if c == 0:
                    w = [ldwv]
                    if prev_ev.get(bank) is not None:
                        w.append(prev_ev[bank])
                mms.append(p.add(
                    "tensor",
                    lambda e, m=m, gi=gi, c=c, psl=psl: e.matmul(
                        psl, ab(bXTB + 4096 * c + 128 * m, 128),
                        ab(bW + 2048 * c + 512 * gi, 512),
                        start=(c == 0), stop=(c == 7)),
                    waits=w, tag="vmm"))
            w2 = [mms[-1]]
            if gi == 0:
                w2.append(vdmas[-1] if vdmas else kdmas[-1])
            ev = p.add("scalar", lambda e, gi=gi, psl=psl: e.copy(ab(bSTG + 512 * gi, 512), psl), waits=w2)
            prev_ev[bank] = ev
            evs.append(ev)
        vdmas.append(p.add(
            "sync",
            lambda e, m=m: e.dma_start(out=vh_r[:, m, :], in_=ab(bSTG, 2048)),
            waits=evs, dma="out"))

    if "stop_a0" in dbg:
        p.add("sync", lambda e: e.nop(), waits=qdmas + kdmas + vdmas)
        return

    bQ = bBIG
    bK = [bBIG + 8192, bBIG + 16384]
    bV = [bBIG + 24576, bBIG + 32768]
    bE1 = [bBIG + 40960, bBIG + 45056, bBIG + 49152]
    bE2 = [bBIG + 53248, bBIG + 57344]
    bPT = bBIG + 61440
    bOST = bPT + 4096
    bOTM = bOST + 1024
    assert bOTM + 512 <= 73728

    fD1 = [fBIG + 1024, fBIG + 1040]
    fD2 = [fBIG + 1028, fBIG + 1044]
    fDEN = [fBIG + 1056, fBIG + 1056 + 16]
    fON = [fBIG + 1100, fBIG + 1100 + 256]
    fSQ = fBIG + 1100 + 512

    qt_r = E["QTh"].rearrange("(c p) t -> p c t", p=128)
    kt_r = E["KTh"].rearrange("(c p) t -> p c t", p=128)

    def load_head(h, buf, waits):
        lq = p.add("sync", lambda e, h=h: e.dma_start(
            out=AB[:, bQ:bQ + 8192].rearrange("p (c t) -> p c t", c=2),
            in_=qt_r[:, 2 * h:2 * h + 2, :]), dma="win",
            waits=waits + (qdmas if h == 0 else []))
        lk = p.add("sync", lambda e, h=h, buf=buf: e.dma_start(
            out=AB[:, bK[buf]:bK[buf] + 8192].rearrange("p (c t) -> p c t", c=2),
            in_=kt_r[:, 2 * h:2 * h + 2, :]), dma="win",
            waits=(kdmas if h == 0 else []))
        lv = p.add("sync", lambda e, h=h, buf=buf: e.dma_start(
            out=AB[:, bV[buf]:bV[buf] + 8192].rearrange("p (t o) -> p t o", t=32),
            in_=vh_r[:, :, 256 * h:256 * h + 256]), dma="win",
            waits=(vdmas if h == 0 else []))
        return [lq, lk, lv]

    NU = H * 32
    st = {
        "loads": {0: load_head(0, 0, [])},
        "exp_last": [None, None],
        "comb": {}, "exps": {}, "dens": {}, "tr_last": {}, "pv": {},
        "epi": {}, "sq": {}, "otm": {}, "otr": {}, "ose": {}, "otdma": {},
        "head_last_exp": {},
        "trev_last": [None, None],
    }
    ot_dmas = []

    def emit_head_prefetch(u):
        h, qt = u // 32, u % 32
        if qt == 2 and h + 1 < H:
            w = []
            if h >= 1:
                w.append(st["pv"][(h - 1) * 32 + 31])
            lk = p.add("sync", lambda e, h1=h + 1, buf=1 - (h % 2): e.dma_start(
                out=AB[:, bK[buf]:bK[buf] + 8192].rearrange("p (c t) -> p c t", c=2),
                in_=kt_r[:, 2 * h1:2 * h1 + 2, :]), dma="win",
                waits=w)
            lv = p.add("sync", lambda e, h1=h + 1, buf=1 - (h % 2): e.dma_start(
                out=AB[:, bV[buf]:bV[buf] + 8192].rearrange("p (t o) -> p t o", t=32),
                in_=vh_r[:, :, 256 * h1:256 * h1 + 256]), dma="win")
            st["loads"][h + 1] = [lk, lv]


    def emit_S_span(u, j):
        h, qt = u // 32, u % 32
        buf = h % 2
        sl = u % 2
        loads = st["loads"][h]
        oQ, oK = bQ, bK[buf]
        exps = st["exps"].setdefault(u, {1: [], 2: []})
        for br, Sp, qoff, koff, bEo, fDx in (
                (1, S1p, oQ, oK, bE1[u % 3], fD1[sl]),
                (2, S2p, oQ + 4096, oK + 4096, bE2[sl], fD2[sl])):
            mms = []
            for half in range(2):
                kb = 2 * j + half
                w = []
                if kb == 0:
                    w = list(loads)
                    if st["exp_last"][br - 1] is not None:
                        w.append(st["exp_last"][br - 1])
                elif half == 0:
                    w.append(exps[br][j - 1])
                mms.append(p.add(
                    "tensor",
                    lambda e, qt=qt, kb=kb, Sp=Sp, qoff=qoff, koff=koff, half=half: e.matmul(
                        Sp[:, 512 * half:512 * half + 512],
                        ab(qoff + 128 * qt, 128), ab(koff + 512 * kb, 512),
                        start=True, stop=True),
                    waits=w, tag=f"s{br}u{u}"))
            ew = list(mms)
            if j == 0:
                if br == 1:
                    if (u - 3) in st["tr_last"]:
                        ew.append(st["tr_last"][u - 3])
                    if (u - 2) in st["dens"]:
                        ew.append(st["dens"][u - 2][0])
                if br == 2 and (u - 2) in st["comb"]:
                    ew.append(st["comb"][u - 2])
            exps[br].append(p.add(
                "scalar",
                lambda e, j=j, Sp=Sp, bEo=bEo, fDx=fDx: e.activation(
                    ab(bEo + 1024 * j, 1024), Sp[:, :], AF.Exp, scale=SCALE,
                    accum_out=af(fDx + j, 1)),
                waits=ew, tag=f"e{br}u{u}"))
            if j == 3:
                st["exp_last"][br - 1] = exps[br][-1]
                if qt == 31 and br == 2:
                    st["head_last_exp"][h] = exps[2][-1]

    def emit_TR(u, ktg_lo, ktg_hi):
        sl = u % 3
        c2 = st["comb"][u]
        for ktg in range(ktg_lo, ktg_hi + 1):
            dst = (trp0, trp1)[ktg % 2]
            trs = []
            for i in range(8):
                kt = 8 * ktg + i
                w = []
                if i == 0:
                    w = [c2, ld_idb]
                    if st["trev_last"][ktg % 2] is not None:
                        w.append(st["trev_last"][ktg % 2])
                trs.append(p.add(
                    "tensor",
                    lambda e, kt=kt, dst=dst, sl=sl: e.transpose(
                        dst[:, 128 * (kt % 8):128 * (kt % 8) + 128],
                        ab(bE1[sl] + 128 * kt, 128), ab(bIDENT, 128)),
                    waits=w, tag=f"tru{u}"))
            w2 = [trs[-1]]
            if (u - 1) in st["pv"]:
                w2.append(st["pv"][u - 1])
            tev = p.add(
                "vector",
                lambda e, ktg=ktg, dst=dst: e.tensor_copy(ab(bPT + 1024 * ktg, 1024), dst[:, :]),
                waits=w2, tag=f"trevu{u}")
            st["trev_last"][ktg % 2] = tev
            st.setdefault("trevs", {}).setdefault(u, []).append(tev)
            if ktg == 3:
                st["tr_last"][u] = trs[-1]

    def emit_PV(u, half):
        h = u // 32
        buf = h % 2
        sl = u % 2
        oV = bV[buf]
        trevs = st["trevs"][u]
        pvs = st.setdefault("pvs", {}).setdefault(u, [])
        for kt in range(16 * half, 16 * half + 16):
            w = []
            if kt % 8 == 0:
                w.append(trevs[kt // 8])
            if kt == 0:
                if (u - 1) in st["epi"]:
                    w.append(st["epi"][u - 1])
                if u == 0:
                    w += [nlam, ev_rs]
            pvs.append(p.add(
                "tensor",
                lambda e, kt=kt, sl=sl, oV=oV: e.matmul(
                    ovp[:, 256 * sl:256 * sl + 256],
                    ab(bPT + 128 * kt, 128), ab(oV + 256 * kt, 256),
                    start=(kt == 0), stop=(kt == 31)),
                waits=w, tag=f"pvu{u}"))
        if half == 1:
            st["pv"][u] = pvs[-1]

    def emit_epilogue(u):
        sl = u % 2
        fDx = fDEN[sl]
        fo = fON[sl]
        ms, r1 = st["dens"][u]
        ew = [st["pv"][u], r1]
        if (u - 2) in st["sq"]:
            ew.append(st["sq"][u - 2])
        epi = p.add("vector", lambda e, sl=sl, fo=fo, fDx=fDx: e.tensor_scalar(
            af(fo, 256), ovp[:, 256 * sl:256 * sl + 256], af(fDx + 4, 1), None, ALU.mult),
            waits=ew, tag=f"epiu{u}")
        st["epi"][u] = epi
        sqm = p.add("vector", lambda e, fo=fo: e.tensor_tensor(
            af(fSQ, 256), af(fo, 256), af(fo, 256), ALU.mult), tag=f"squ{u}")
        st["sq"][u] = sqm
        t2 = p.add("vector", lambda e, fDx=fDx: e.tensor_scalar(
            af(fSQ, 256), af(fSQ, 256), 1.0 / DV, EPS, ALU.mult, ALU.add,
            accum_out=af(fDx + 7, 1)), tag=f"t2u{u}")
        ln_ = p.add("scalar", lambda e, fDx=fDx: e.activation(af(fDx + 8, 1), af(fDx + 7, 1), AF.Ln), waits=[t2])
        rinv = p.add("scalar", lambda e, fDx=fDx: e.activation(af(fDx + 9, 1), af(fDx + 8, 1), AF.Exp, scale=-0.5), tag=f"rinvu{u}")
        p.add("vector", lambda e, fo=fo, fDx=fDx: e.tensor_scalar(af(fo, 256), af(fo, 256), af(fDx + 9, 1), None, ALU.mult), waits=[rinv])
        otm_w = [ev_rs]
        if (u - 2) in st["otr"]:
            otm_w.append(st["otr"][u - 2][-1])
        otm = p.add("vector", lambda e, sl=sl, fo=fo: e.tensor_tensor(
            ab(bOTM + 256 * sl, 256), af(fo, 256), af(fRMSB, 256), ALU.mult), waits=otm_w, tag=f"otmu{u}")
        st["otm"][u] = otm

    def emit_S_dve(u):
        h, qt = u // 32, u % 32
        sl = u % 2
        fDx = fDEN[sl]
        exps = st["exps"][u]
        if qt == 31 and h + 1 < H:
            lq = p.add("sync", lambda e, h1=h + 1: e.dma_start(
                out=AB[:, bQ:bQ + 8192].rearrange("p (c t) -> p c t", c=2),
                in_=qt_r[:, 2 * h1:2 * h1 + 2, :]), dma="win",
                waits=[st["head_last_exp"][h]])
            st["loads"][h + 1].append(lq)
        p.add("vector", lambda e, sl=sl, fDx=fDx: e.tensor_reduce(
            af(fDx, 2), AFr[:, fD1[sl]:fD1[sl] + 8].rearrange("p (b x) -> p b x", b=2),
            AX.X, ALU.add), waits=exps[1] + exps[2])
        rd = p.add("vector", lambda e, fDx=fDx: e.reciprocal(af(fDx + 4, 2), af(fDx, 2)))
        p.add("vector", lambda e, fDx=fDx: e.tensor_tensor(af(fDx + 3, 1), af(fDx, 1), af(fDx + 5, 1), ALU.mult))
        ms = p.add("vector", lambda e, h=h, fDx=fDx: e.tensor_tensor(af(fDx + 6, 1), af(fDx + 3, 1), af(fLAM + h, 1), ALU.mult), waits=[nlam])
        st["dens"][u] = (ms, rd)
        c1 = p.add("vector", lambda e, sl=sl, fDx=fDx: e.tensor_scalar(ab(bE2[sl], 4096), ab(bE2[sl], 4096), af(fDx + 6, 1), None, ALU.mult), tag=f"c1u{u}")
        c2 = p.add("gpsimd", lambda e, sl=sl, s3=u % 3: e.tensor_tensor(ab(bE1[s3], 4096), ab(bE1[s3], 4096), ab(bE2[sl], 4096), ALU.add),
                   waits=[c1, exps[1][-1]], tag=f"c2u{u}")
        st["comb"][u] = c2

    def stage_OTR(u):
        h, qt = u // 32, u % 32
        sl = u % 2
        otrs = []
        for chv in range(2):
            w = [st["otm"][u]]
            if qt % 4 == 0 and chv == 0 and (u - 4) in st["ose"]:
                w += st["ose"][u - 4]
            otrs.append(p.add(
                "tensor",
                lambda e, qt=qt, chv=chv, sl=sl: e.transpose(
                    pob[:, 256 * (qt % 4) + 128 * chv:256 * (qt % 4) + 128 * chv + 128],
                    ab(bOTM + 256 * sl + 128 * chv, 128), ab(bIDENT, 128)),
                waits=w, tag=f"otru{u}"))
        st["otr"][u] = otrs
        if qt % 4 == 3:
            qb = u - 3
            grp = (qt - 3) // 4
            pob_v = pob[:, :].rearrange("p (q x) -> p q x", q=4)
            oses = []
            all_otrs = [o for i in range(4) for o in st["otr"][u - 3 + i]]
            for ch in range(2):
                w = list(all_otrs)
                prevd = st["otdma"].get((h, grp - 1), st["otdma"].get((h - 1, 7)))
                if prevd is not None:
                    w.append(prevd[ch])
                oses.append(p.add(
                    "vector",
                    lambda e, ch=ch: e.tensor_copy(
                        AB[:, bOST + 512 * ch:bOST + 512 * ch + 512].rearrange("p (q x) -> p q x", q=4),
                        pob_v[:, :, 128 * ch:128 * ch + 128]),
                    waits=w, tag=f"oseu{u}"))
            st["ose"][qb] = oses
            pair = []
            for ch in range(2):
                dma = p.add(
                    "sync",
                    lambda e, h=h, grp=grp, ch=ch: e.dma_start(
                        out=E["OT"][h * 256 + ch * 128:h * 256 + ch * 128 + 128, grp * 512:grp * 512 + 512],
                        in_=ab(bOST + 512 * ch, 512)),
                    waits=[oses[ch]], dma="out", tag=f"otdma{h}.{grp}")
                ot_dmas.append(dma)
                pair.append(dma)
            st["otdma"][(h, grp)] = pair

    nheads = H
    for d in dbg:
        if isinstance(d, str) and d.startswith("nh"):
            nheads = int(d[2:])
    NU = nheads * 32

    for u in range(NU + 4):
        cur = u < NU
        prv = 2 <= u <= NU + 1
        if cur:
            emit_head_prefetch(u)
            emit_S_span(u, 0)
        if 4 <= u <= NU + 3:
            stage_OTR(u - 4)
        if prv:
            emit_TR(u - 2, 0, 0)
        if cur:
            emit_S_span(u, 1)
        if prv:
            emit_TR(u - 2, 1, 2)
        if cur:
            emit_S_span(u, 2)
        if prv:
            emit_TR(u - 2, 3, 3)
            emit_PV(u - 2, 0)
        if cur:
            emit_S_span(u, 3)
        if prv:
            emit_PV(u - 2, 1)
            emit_epilogue(u - 2)
        if cur:
            emit_S_dve(u)

    if "stop_a1" in dbg:
        p.add("sync", lambda e: e.nop(), waits=ot_dmas)
        return

    bWO = bBIG
    bOB = [bWO + 16384, bWO + 24576]
    bRB = bWO + 32768
    bRSQ = bRB + 4096
    bX1B = bRSQ + 4096
    assert bX1B + 4096 <= 73728
    fXT = fBIG + 1024
    fR = fXT + 4096
    fX1 = fR + 4096
    fT = fX1 + 4096
    assert fT + 1536 <= 15488

    ldwo = p.add("gpsimd", lambda e: e.dma_start(out=ab(bWO, 16384), in_=E["WoT"]),
        dma="wing", waits=[st["head_last_exp"][H - 1], st["pv"][NU - 1]])
    ot_r = E["OT"].rearrange("(c p) t -> p c t", p=128)
    xt_r = E["XT"].rearrange("(c p) t -> p c t", p=128)
    x1f_r = E["X1F"].rearrange("(c p) t -> p c t", p=128)
    x1b_r = E["X1B"].rearrange("(c p) t -> p c t", p=128)

    prev = {}
    x1_dmas = []
    for n in range(8):
        bufn = n % 2
        lob_w = list(ot_dmas) if n == 0 else ([prev[("womm", n - 2)]] if ("womm", n - 2) in prev else [])
        lob = p.add("sync", lambda e, n=n, bufn=bufn: e.dma_start(
            out=AB[:, bOB[bufn]:bOB[bufn] + 8192].rearrange("p (c t) -> p c t", c=16),
            in_=ot_r[:, :, 512 * n:512 * n + 512]), dma="win", waits=lob_w)
        lxt = p.add("sync", lambda e, n=n: e.dma_start(
            out=AFr[:, fXT:fXT + 4096].rearrange("p (c t) -> p c t", c=8),
            in_=xt_r[:, :, 512 * n:512 * n + 512]), dma="win",
            waits=[prev[("rev_last", n - 1)]] if ("rev_last", n - 1) in prev else [])
        revs = []
        for m in range(8):
            psl = (S1p[:, 0:512], S1p[:, 512:1024])[m % 2]
            mms = []
            for ec in range(16):
                w = []
                if ec == 0:
                    w = [lob, ldwo]
                    if ("rev", n, m - 2) in prev:
                        w.append(prev[("rev", n, m - 2)])
                    if m < 2 and ("rev", n - 1, 6 + m) in prev:
                        w.append(prev[("rev", n - 1, 6 + m)])
                mms.append(p.add(
                    "tensor",
                    lambda e, m=m, ec=ec, psl=psl, bufn=bufn: e.matmul(
                        psl, ab(bWO + 1024 * ec + 128 * m, 128), ab(bOB[bufn] + 512 * ec, 512),
                        start=(ec == 0), stop=(ec == 15)),
                    waits=w, tag=f"wo{n}.{m}"))
            prev[("womm", n)] = mms[-1]
            rev = p.add("vector", lambda e, m=m, psl=psl: e.tensor_tensor(
                af(fR + 512 * m, 512), psl, af(fXT + 512 * m, 512), ALU.add),
                waits=[mms[-1], lxt], tag=f"rev{n}.{m}")
            prev[("rev", n, m)] = rev
            revs.append(rev)
        rbfs, rsqs = [], []
        for m in range(8):
            rbfs.append(p.add("scalar", lambda e, m=m: e.copy(ab(bRB + 512 * m, 512), af(fR + 512 * m, 512)), waits=[revs[m]]))
            rsqs.append(p.add("scalar", lambda e, m=m: e.activation(ab(bRSQ + 512 * m, 512), af(fR + 512 * m, 512), AF.Square), waits=[revs[m]]))
        summ, sqmm = [], []
        for c in range(8):
            w1 = [rbfs[c], ms_ones]
            w2 = [rsqs[c]]
            if c == 0 and ("stat", n - 1) in prev:
                w1.append(prev[("stat", n - 1)])
                w2.append(prev[("stat", n - 1)])
            summ.append(p.add("tensor", lambda e, c=c: e.matmul(
                S2p[:, 0:512], ab(bONES, 128), ab(bRB + 512 * c, 512), start=(c == 0), stop=(c == 7)), waits=w1))
            sqmm.append(p.add("tensor", lambda e, c=c: e.matmul(
                S2p[:, 512:1024], ab(bONES, 128), ab(bRSQ + 512 * c, 512), start=(c == 0), stop=(c == 7)), waits=w2))
        mb = p.add("vector", lambda e: e.tensor_scalar(af(fT, 512), S2p[:, 0:512], 1.0 / D, None, ALU.mult), waits=[summ[-1]])
        t = p.add("vector", lambda e: e.tensor_scalar(af(fT + 512, 512), S2p[:, 512:1024], 1.0 / D, EPS, ALU.mult, ALU.add), waits=[sqmm[-1]])
        prev[("stat", n)] = t
        p.add("vector", lambda e: e.tensor_tensor(af(fT + 1024, 512), af(fT, 512), af(fT, 512), ALU.mult))
        var = p.add("vector", lambda e: e.tensor_tensor(af(fT + 1024, 512), af(fT + 512, 512), af(fT + 1024, 512), ALU.subtract))
        lnv = p.add("scalar", lambda e: e.activation(af(fT + 512, 512), af(fT + 1024, 512), AF.Ln), waits=[var])
        rstd = p.add("scalar", lambda e: e.activation(af(fT + 1024, 512), af(fT + 512, 512), AF.Exp, scale=-0.5), waits=[lnv])
        x1f_ops, x1b_ops = [], []
        for m in range(8):
            wv = [rstd]
            if ("x1dma", n - 1) in prev and m == 0:
                wv.append(prev[("x1dma", n - 1)])
            p.add("vector", lambda e, m=m: e.tensor_tensor(af(fX1 + 512 * m, 512), af(fR + 512 * m, 512), af(fT, 512), ALU.subtract), waits=wv)
            x1c = p.add("vector", lambda e, m=m: e.tensor_tensor(af(fX1 + 512 * m, 512), af(fX1 + 512 * m, 512), af(fT + 1024, 512), ALU.mult))
            wb = [x1c]
            if m == 0 and ("x1dmab", n - 1) in prev:
                wb.append(prev[("x1dmab", n - 1)])
            x1b = p.add("scalar", lambda e, m=m: e.copy(ab(bX1B + 512 * m, 512), af(fX1 + 512 * m, 512)), waits=wb)
            x1f_ops.append(x1c)
            x1b_ops.append(x1b)
        rl = p.add("vector", lambda e: e.tensor_copy(af(fLAM, 1), af(fLAM, 1)))
        prev[("rev_last", n)] = rl
        d1 = p.add("sync", lambda e, n=n: e.dma_start(
            out=x1f_r[:, :, 512 * n:512 * n + 512],
            in_=AFr[:, fX1:fX1 + 4096].rearrange("p (c t) -> p c t", c=8)),
            waits=x1f_ops, dma="out")
        d2 = p.add("sync", lambda e, n=n: e.dma_start(
            out=x1b_r[:, :, 512 * n:512 * n + 512],
            in_=AB[:, bX1B:bX1B + 4096].rearrange("p (c t) -> p c t", c=8)),
            waits=x1b_ops, dma="out")
        prev[("x1dma", n)] = d1
        prev[("x1dmab", n)] = d2
        x1_dmas += [d1, d2]

    if "stop_b" in dbg:
        p.add("sync", lambda e: e.nop(), waits=x1_dmas)
        return

    bXB = bBIG
    bRACC = bXB + 8192
    bINT = bRACC + 8192
    bWG = [bINT + 8192, bINT + 20480, bINT + 32768]
    bG = bINT + 45056
    bCT = bG + 512
    bRW = bCT + 1024
    bCF = bRW + 128
    bLNB = bINT
    assert bCF + 16 <= 73728
    fCB = fBIG + 1024
    fLG = fCB + 1024
    fLGT = fLG + 1024
    fPRB = fLGT + 128
    fCFT = fPRB + 112
    fX1Q = fCFT + 16
    fR2 = fX1Q + 4096
    fT2 = fR2 + 4096
    assert fT2 + 1536 <= 15488

    ldrw = p.add("gpsimd", lambda e: e.dma_start(out=ab(bRW, 128), in_=E["rw"]), dma="wing",
        waits=[prev[("womm", 7)]])

    y_dmas = []
    cp = {}
    for Q in range(4):
        tq = 1024 * Q
        lxb = p.add("sync", lambda e, tq=tq: e.dma_start(
            out=AB[:, bXB:bXB + 8192].rearrange("p (c t) -> p c t", c=8),
            in_=x1b_r[:, :, tq:tq + 1024]), dma="win",
            waits=([prev[("x1dmab", n) ] for n in range(8)] if Q == 0 else [cp[("edn_last",)]]))
        lgevs = []
        for tt in range(2):
            mms = []
            for c in range(8):
                w = []
                if c == 0:
                    w = [lxb, ldrw]
                    if tt == 1:
                        w.append(lgevs[0])
                    elif ("ovp_free",) in cp:
                        w.append(cp[("ovp_free",)])
                    elif Q == 0:
                        w.append(st["epi"].get(NU - 1))
                mms.append(p.add("tensor", lambda e, tt=tt, c=c: e.matmul(
                    ovp[0:16, 0:512], ab(bRW + 16 * c, 16, 0, 128), ab(bXB + 1024 * c + 512 * tt, 512),
                    start=(c == 0), stop=(c == 7)), waits=w))
            lgevs.append(p.add("vector", lambda e, tt=tt: e.tensor_copy(af(fLG + 512 * tt, 512, 0, 16), ovp[0:16, 0:512]), waits=[mms[-1]]))
        trls = []
        for j in range(8):
            w = [lgevs[-1]] if j == 0 else []
            trls.append(p.add("tensor", lambda e, j=j: e.transpose(
                ovp[:, 16 * j:16 * j + 16], af(fLG + 128 * j, 128, 0, 16), af(fIDENT, 16, 0, 16)), waits=w))
        lgt = p.add("vector", lambda e: e.tensor_copy(af(fLGT, 128), ovp[:, 0:128]), waits=[trls[-1]])
        ctevs = []
        last_ovp_reader = lgt
        for j in range(8):
            m1r = p.add("vector", lambda e, j=j: e.reduce_max(af(fPRB, 1), af(fLGT + 16 * j, 16), AX.X))
            nm1 = p.add("vector", lambda e: e.tensor_scalar(af(fPRB + 5, 1), af(fPRB, 1), -1.0, None, ALU.mult))
            ex = p.add("scalar", lambda e, j=j: e.activation(af(fPRB + 16, 16), af(fLGT + 16 * j, 16), AF.Exp, bias=af(fPRB + 5, 1), accum_out=af(fPRB + 1, 1)), waits=[nm1])
            p.add("vector", lambda e: e.reciprocal(af(fPRB + 2, 1), af(fPRB + 1, 1)), waits=[ex])
            p.add("vector", lambda e: e.tensor_scalar(af(fPRB + 32, 16), af(fPRB + 16, 16), af(fPRB + 2, 1), None, ALU.mult))
            p.add("vector", lambda e: e.reduce_max(af(fPRB + 3, 1), af(fPRB + 32, 16), AX.X))
            p.add("vector", lambda e: e.tensor_scalar(af(fPRB + 48, 16), af(fPRB + 32, 16), af(fPRB + 3, 1), None, ALU.is_equal))
            p.add("vector", lambda e: e.tensor_tensor(af(fPRB + 64, 16), af(fPRB + 32, 16), af(fPRB + 48, 16), ALU.mult))
            p.add("vector", lambda e: e.tensor_tensor(af(fPRB + 64, 16), af(fPRB + 32, 16), af(fPRB + 64, 16), ALU.subtract))
            p.add("vector", lambda e: e.reduce_max(af(fPRB + 4, 1), af(fPRB + 64, 16), AX.X))
            p.add("vector", lambda e: e.tensor_scalar(af(fPRB + 48, 16), af(fPRB + 32, 16), af(fPRB + 4, 1), None, ALU.is_ge))
            cf = p.add("vector", lambda e: e.tensor_tensor(af(fCFT, 16), af(fPRB + 32, 16), af(fPRB + 48, 16), ALU.mult))
            cfb = p.add("scalar", lambda e: e.copy(ab(bCF, 16), af(fCFT, 16)), waits=[cf])
            trc = p.add("tensor", lambda e, j=j: e.transpose(trp0[0:16, 0:128], ab(bCF, 16), ab(bIDENT, 128)), waits=[cfb, (ctevs[-1] if ctevs else None)])
            ctevs.append(p.add("vector", lambda e, j=j: e.tensor_copy(ab(bCT + 128 * j, 128, 0, NE), trp0[0:16, 0:128]), waits=[trc]))
        lsg = p.add("gpsimd", lambda e: e.dma_start(out=ab(bWG[0], 8192), in_=E["sgw"]), dma="wing",
            waits=([ldrw, prev[("x1dmab", 7)], prev[("x1dma", 7)]] if Q == 0 else [cp[("edn_last",)]]))
        lsu = p.add("gpsimd", lambda e: e.dma_start(out=ab(bWG[1], 8192), in_=E["suw"]), dma="wing")
        last_sil = cp.get(("lnb_pe_done",))
        last_mi = None
        gu_done = []
        for fm in range(8):
            for tt in range(2):
                gm = []
                for c in range(8):
                    w = []
                    if c == 0:
                        w = [lsg, lxb]
                        if last_sil is not None:
                            w.append(last_sil)
                    gm.append(p.add("tensor", lambda e, fm=fm, tt=tt, c=c: e.matmul(
                        S1p[:, 0:512], ab(bWG[0] + 1024 * c + 128 * fm, 128), ab(bXB + 1024 * c + 512 * tt, 512),
                        start=(c == 0), stop=(c == 7)), waits=w))
                sil = p.add("scalar", lambda e: e.activation(ab(bG + 16, 496) if False else ab(bG, 512), S1p[:, 0:512], AF.Silu), waits=[gm[-1], (last_mi if last_mi is not None else None)], tag=f"sil{Q}.{fm}.{tt}")
                last_sil = sil
                um = []
                for c in range(8):
                    w = []
                    if c == 0:
                        w = [lsu]
                        if last_mi is not None:
                            w.append(last_mi)
                    um.append(p.add("tensor", lambda e, fm=fm, tt=tt, c=c: e.matmul(
                        S1p[:, 512:1024], ab(bWG[1] + 1024 * c + 128 * fm, 128), ab(bXB + 1024 * c + 512 * tt, 512),
                        start=(c == 0), stop=(c == 7)), waits=w))
                mi = p.add("vector", lambda e, fm=fm, tt=tt: e.tensor_tensor(
                    ab(bINT + 1024 * fm + 512 * tt, 512), ab(bG, 512), S1p[:, 512:1024], ALU.mult),
                    waits=[um[-1], sil], tag=f"smi{Q}.{fm}.{tt}")
                last_mi = mi
                gu_done.append(mi)
        lsd = p.add("gpsimd", lambda e: e.dma_start(out=ab(bWG[0], 8192), in_=E["sdw"]), dma="wing", waits=[gu_done[-1], last_sil])
        last_rc = [cp.get(("lnb_pe_done",)), cp.get(("lnb_pe_done",))]
        dn_ops = []
        for dm in range(8):
            for tt in range(2):
                bank = tt
                psl = (S2p[:, 0:512], S2p[:, 512:1024])[bank]
                mm = []
                for c in range(8):
                    w = []
                    if c == 0:
                        w = [lsd] + gu_done[tt::2]
                        if last_rc[bank] is not None:
                            w.append(last_rc[bank])
                    mm.append(p.add("tensor", lambda e, dm=dm, tt=tt, c=c, psl=psl: e.matmul(
                        psl, ab(bWG[0] + 1024 * c + 128 * dm, 128), ab(bINT + 1024 * c + 512 * tt, 512),
                        start=(c == 0), stop=(c == 7)), waits=w))
                rc = p.add("vector", lambda e, dm=dm, tt=tt, psl=psl: e.tensor_copy(
                    ab(bRACC + 1024 * dm + 512 * tt, 512), psl), waits=[mm[-1]], tag=f"sracc{Q}.{dm}.{tt}")
                last_rc[bank] = rc
                dn_ops.append(rc)
        for ei in range(NE):
            wbuf = ei % 3
            if ei >= 3:
                wld_w = [cp[("emi_last", ei - 3)], cp[("edn_mm_last", ei - 3)]]
            elif ei == 0:
                wld_w = [dn_ops[-1]]
            elif ei == 1:
                wld_w = [gu_done[-1], dn_ops[-1]]
            else:
                wld_w = [w for w in (cp.get(("emi_last", 14)), cp.get(("edn_mm_last", 14))) if w is not None]
            lg_ = p.add("gpsimd", lambda e, ei=ei, wbuf=wbuf: e.dma_start(
                out=ab(bWG[wbuf], 4096), in_=E["gw"][ei]), dma="wing", waits=wld_w)
            lu_ = p.add("gpsimd", lambda e, ei=ei, wbuf=wbuf: e.dma_start(
                out=ab(bWG[wbuf] + 4096, 4096), in_=E["uw"][ei]), dma="wing")
            ld_ = p.add("gpsimd", lambda e, ei=ei, wbuf=wbuf: e.dma_start(
                out=ab(bWG[wbuf] + 8192, 4096), in_=E["dw"][ei]), dma="wing")
            cbevs = []
            for tt in range(2):
                w = [ld_sel, ctevs[-1], last_ovp_reader]
                cbm = p.add("tensor", lambda e, ei=ei, tt=tt: e.matmul(
                    ovp[:, 0:512], ab(bSEL + 128 * ei, 128, 0, NE), ab(bCT + 512 * tt, 512, 0, NE),
                    start=True, stop=True), waits=w)
                cbev = p.add("vector", lambda e, tt=tt: e.tensor_copy(af(fCB + 512 * tt, 512), ovp[:, 0:512]), waits=[cbm])
                cbevs.append(cbev)
                last_ovp_reader = cbev
            emi = []
            last_esil = cp.get(("esil_last",))
            last_emi = cp.get(("emi_bank_last",))
            for fm in range(4):
                for tt in range(2):
                    gm = []
                    for c in range(8):
                        w = []
                        if c == 0:
                            w = [lg_]
                            if ei == 0 and fm == 0 and tt == 0:
                                w.append(dn_ops[-1])
                                w.append(gu_done[-1])
                            if last_esil is not None:
                                w.append(last_esil)
                        gm.append(p.add("tensor", lambda e, fm=fm, tt=tt, c=c, wbuf=wbuf: e.matmul(
                            S1p[:, 0:512], ab(bWG[wbuf] + 512 * c + 128 * fm, 128), ab(bXB + 1024 * c + 512 * tt, 512),
                            start=(c == 0), stop=(c == 7)), waits=w))
                    sil = p.add("scalar", lambda e: e.activation(ab(bG, 512), S1p[:, 0:512], AF.Silu),
                                waits=[gm[-1], (last_emi if last_emi is not None else None)])
                    last_esil = sil
                    um = []
                    for c in range(8):
                        w = []
                        if c == 0:
                            w = [lu_]
                            if last_emi is not None:
                                w.append(last_emi)
                        um.append(p.add("tensor", lambda e, fm=fm, tt=tt, c=c, wbuf=wbuf: e.matmul(
                            S1p[:, 512:1024], ab(bWG[wbuf] + 4096 + 512 * c + 128 * fm, 128), ab(bXB + 1024 * c + 512 * tt, 512),
                            start=(c == 0), stop=(c == 7)), waits=w))
                    mi1 = p.add("vector", lambda e, fm=fm, tt=tt: e.tensor_tensor(
                        ab(bINT + 1024 * fm + 512 * tt, 512), ab(bG, 512), S1p[:, 512:1024], ALU.mult),
                        waits=[um[-1], sil])
                    mi2 = p.add("vector", lambda e, fm=fm, tt=tt: e.tensor_tensor(
                        ab(bINT + 1024 * fm + 512 * tt, 512), ab(bINT + 1024 * fm + 512 * tt, 512), af(fCB + 512 * tt, 512), ALU.mult),
                        waits=[cbevs[tt]], tag=f"emi{Q}.{ei}.{fm}.{tt}")
                    last_emi = mi1
                    emi.append(mi2)
            cp[("esil_last",)] = last_esil
            cp[("emi_bank_last",)] = last_emi
            cp[("emi_last", ei)] = emi[-1]
            edn = []
            for dm in range(8):
                for tt in range(2):
                    bank = tt
                    psl = (S2p[:, 0:512], S2p[:, 512:1024])[bank]
                    mm = []
                    for c in range(4):
                        w = []
                        if c == 0:
                            w = [ld_] + emi[tt::2]
                            if last_rc[bank] is not None:
                                w.append(last_rc[bank])
                        mm.append(p.add("tensor", lambda e, dm=dm, tt=tt, c=c, psl=psl, wbuf=wbuf: e.matmul(
                            psl, ab(bWG[wbuf] + 8192 + 1024 * c + 128 * dm, 128), ab(bINT + 1024 * c + 512 * tt, 512),
                            start=(c == 0), stop=(c == 3)), waits=w))
                    rc = p.add("vector", lambda e, dm=dm, tt=tt, psl=psl: e.tensor_tensor(
                        ab(bRACC + 1024 * dm + 512 * tt, 512), ab(bRACC + 1024 * dm + 512 * tt, 512), psl, ALU.add),
                        waits=[mm[-1]], tag=f"eracc{Q}.{ei}.{dm}.{tt}")
                    last_rc[bank] = rc
                    edn.append(rc)
            cp[("edn_mm_last", ei)] = mm[-1]
            cp[("edn_last",)] = edn[-1]
        cp[("ovp_free",)] = last_ovp_reader
        lnb_pe = None
        for tt in range(2):
            if Q == 0 and tt == 0:
                lx1_w = [prev[("x1dma", n)] for n in range(8)]
            elif tt == 1:
                lx1_w = [cp[("ydma", Q, 0)]]
            else:
                lx1_w = [cp[("ydma", Q - 1, 1)]]
            lx1 = p.add("sync", lambda e, tq=tq, tt=tt: e.dma_start(
                out=AFr[:, fX1Q:fX1Q + 4096].rearrange("p (c t) -> p c t", c=8),
                in_=x1f_r[:, :, tq + 512 * tt:tq + 512 * tt + 512]), dma="win", waits=lx1_w)
            r2s = []
            for m in range(8):
                r2s.append(p.add("vector", lambda e, m=m, tt=tt: e.tensor_tensor(
                    af(fR2 + 512 * m, 512), af(fX1Q + 512 * m, 512), ab(bRACC + 1024 * m + 512 * tt, 512), ALU.add),
                    waits=[lx1, cp[("edn_last",)]], tag=f"r2.{Q}.{tt}.{m}"))
            rbfs, rsqs = [], []
            for m in range(8):
                w = [r2s[m]]
                if tt == 0 and m == 0:
                    w.append(cp[("emi_last", NE - 1)])
                rbfs.append(p.add("scalar", lambda e, m=m: e.copy(ab(bLNB + 512 * m, 512), af(fR2 + 512 * m, 512)), waits=w))
                rsqs.append(p.add("scalar", lambda e, m=m: e.activation(ab(bLNB + 4096 + 512 * m, 512), af(fR2 + 512 * m, 512), AF.Square), waits=[r2s[m]]))
            summ, sqmm = [], []
            for c in range(8):
                w1 = [rbfs[c], ms_ones]
                w2 = [rsqs[c]]
                if c == 0:
                    w1.append(cp[("edn_mm_last", NE - 1)])
                    if ("lnstat", Q, tt - 1) in cp:
                        w1.append(cp[("lnstat", Q, tt - 1)])
                        w2.append(cp[("lnstat", Q, tt - 1)])
                    elif ("lnstat", Q - 1, 1) in cp:
                        w1.append(cp[("lnstat", Q - 1, 1)])
                        w2.append(cp[("lnstat", Q - 1, 1)])
                summ.append(p.add("tensor", lambda e, c=c: e.matmul(
                    S1p[:, 0:512], ab(bONES, 128), ab(bLNB + 512 * c, 512), start=(c == 0), stop=(c == 7)), waits=w1))
                sqmm.append(p.add("tensor", lambda e, c=c: e.matmul(
                    S1p[:, 512:1024], ab(bONES, 128), ab(bLNB + 4096 + 512 * c, 512), start=(c == 0), stop=(c == 7)), waits=w2))
            mb = p.add("vector", lambda e: e.tensor_scalar(af(fT2, 512), S1p[:, 0:512], 1.0 / D, None, ALU.mult), waits=[summ[-1]])
            t = p.add("vector", lambda e: e.tensor_scalar(af(fT2 + 512, 512), S1p[:, 512:1024], 1.0 / D, EPS, ALU.mult, ALU.add), waits=[sqmm[-1]])
            cp[("lnstat", Q, tt)] = t
            p.add("vector", lambda e: e.tensor_tensor(af(fT2 + 1024, 512), af(fT2, 512), af(fT2, 512), ALU.mult))
            var = p.add("vector", lambda e: e.tensor_tensor(af(fT2 + 1024, 512), af(fT2 + 512, 512), af(fT2 + 1024, 512), ALU.subtract))
            lnv = p.add("scalar", lambda e: e.activation(af(fT2 + 512, 512), af(fT2 + 1024, 512), AF.Ln), waits=[var])
            rstd = p.add("scalar", lambda e: e.activation(af(fT2 + 1024, 512), af(fT2 + 512, 512), AF.Exp, scale=-0.5), waits=[lnv])
            yevs = []
            for m in range(8):
                p.add("vector", lambda e, m=m: e.tensor_tensor(af(fX1Q + 512 * m, 512), af(fR2 + 512 * m, 512), af(fT2, 512), ALU.subtract), waits=[rstd])
                yevs.append(p.add("vector", lambda e, m=m: e.tensor_tensor(af(fX1Q + 512 * m, 512), af(fX1Q + 512 * m, 512), af(fT2 + 1024, 512), ALU.mult)))
            yd = p.add("sync", lambda e, tq=tq, tt=tt: e.dma_start(
                out=E["YT"].rearrange("(c p) t -> p c t", p=128)[:, :, tq + 512 * tt:tq + 512 * tt + 512],
                in_=AFr[:, fX1Q:fX1Q + 4096].rearrange("p (c t) -> p c t", c=8)),
                waits=yevs, dma="out")
            cp[("ydma", Q, tt)] = yd
            y_dmas.append(yd)
        cp[("lnb_pe_done",)] = cp[("lnstat", Q, 1)]

    p.add("sync", lambda e: e.nop(), waits=y_dmas)


def make_sel():
    s = np.zeros((NE, NE * 128), dtype=np.float32)
    for e in range(NE):
        s[e, e * 128:(e + 1) * 128] = 1.0
    return s


def swz(m):
    m = np.asarray(m, dtype=np.float32)
    c = m.shape[0] // 128
    return np.ascontiguousarray(m.reshape(c, 128, m.shape[1]).transpose(1, 0, 2).reshape(128, -1))


def make_inputs_for_core(inputs, b):
    f32 = lambda x: np.ascontiguousarray(np.asarray(x, dtype=np.float32))
    d = {}
    d["XT"] = f32(np.asarray(inputs["L0_attn_X"][b]).T)
    d["WqT"] = swz(np.asarray(inputs["L0_attn_Wq"]).T)
    d["WkT"] = swz(np.asarray(inputs["L0_attn_Wk"]).T)
    d["WvT"] = swz(np.asarray(inputs["L0_attn_Wv"]).T)
    d["WoT"] = swz(np.asarray(inputs["L0_attn_Wo"]).T)
    d["lq1"] = f32(inputs["L0_attn_lambda_q1"])
    d["lk1"] = f32(inputs["L0_attn_lambda_k1"])
    d["lq2"] = f32(inputs["L0_attn_lambda_q2"])
    d["lk2"] = f32(inputs["L0_attn_lambda_k2"])
    d["rms_scale"] = f32(np.asarray(inputs["L0_attn_rms_scale"])[None, :])
    d["rw"] = swz(np.asarray(inputs["L0_ffn_router_w"]))
    d["gw"] = np.stack([swz(np.asarray(inputs["L0_ffn_gate_w"])[e]) for e in range(NE)])
    d["uw"] = np.stack([swz(np.asarray(inputs["L0_ffn_up_w"])[e]) for e in range(NE)])
    d["dw"] = np.stack([swz(np.asarray(inputs["L0_ffn_down_w"])[e]) for e in range(NE)])
    d["sgw"] = swz(np.asarray(inputs["L0_ffn_shared_gate_w"]))
    d["suw"] = swz(np.asarray(inputs["L0_ffn_shared_up_w"]))
    d["sdw"] = swz(np.asarray(inputs["L0_ffn_shared_down_w"]))
    d["identf"] = np.eye(128, dtype=np.float32)
    d["onesf"] = np.ones((1, 128), dtype=np.float32)
    d["self"] = make_sel()
    return d


def kernel(**inputs):
    nc = build_program()
    in_maps = [make_inputs_for_core(inputs, b) for b in range(8)]
    res = run_bass_kernel_spmd(nc, in_maps, core_ids=list(range(8)))
    out = np.stack([np.asarray(r["YT"]).T for r in res.results], axis=0)
    return np.ascontiguousarray(out).astype(np.float32)


# version 11
# speedup vs baseline: 1.0244x; 1.0244x over previous
import math
import numpy as np
import ml_dtypes
import concourse.bass as bass
import concourse.mybir as mybir
from concourse.bass_utils import run_bass_kernel_spmd

F32 = mybir.dt.float32
BF16 = mybir.dt.bfloat16
AF = mybir.ActivationFunctionType
ALU = mybir.AluOpType
AX = mybir.AxisListType

T = 4096
D = 1024
H = 8
DV = 256
NE = 16
FE = 512
FS = 1024
SCALE = 1.0 / math.sqrt(128.0)
LAMBDA_INIT = 0.8 - 0.6 * math.exp(-0.3)
EPS = 1e-5

ENGINES = ("tensor", "scalar", "vector", "gpsimd", "sync")


DMA_POOL = 8


class Op:
    __slots__ = ("eng", "fn", "waits", "dma", "marked", "sem", "val", "pre", "tag")

    def __init__(self, eng, fn, waits, dma, tag):
        self.eng, self.fn, self.dma, self.tag = eng, fn, dma, tag
        self.waits = [w for w in waits if w is not None]
        self.marked = False
        self.sem = None
        self.val = None
        self.pre = None


class Prog:
    def __init__(self):
        self.ops = []

    def add(self, eng, fn, waits=(), dma=None, tag=""):
        op = Op(eng, fn, waits, dma, tag)
        self.ops.append(op)
        return op

    def resolve(self, nc, ctx):
        groups = sorted({op.dma for op in self.ops if op.dma is not None})
        sems = {}
        for e in ENGINES:
            sems["e_" + e] = ctx.enter_context(nc.semaphore("e_" + e))
        pool = {}
        for g in groups:
            pool[g] = [ctx.enter_context(nc.semaphore(f"d_{g}{i}")) for i in range(DMA_POOL)]
        CHAIN = ("vector", "scalar")
        for op in self.ops:
            kept = []
            for w in op.waits:
                if w.dma is None and w.eng == op.eng and op.eng not in CHAIN:
                    continue
                if w.dma is None and w.eng == op.eng and op.eng in CHAIN:
                    continue
                w.marked = True
                kept.append(w)
            op.waits = kept
        for op in self.ops:
            if op.eng in CHAIN and op.dma is None:
                op.marked = True
        counts = {("e_" + e): 0 for e in ENGINES}
        rr = {g: 0 for g in groups}
        dma_counts = {}
        for op in self.ops:
            if op.dma is not None:
                op.marked = True
                i = rr[op.dma]
                rr[op.dma] = (i + 1) % DMA_POOL
                sem = pool[op.dma][i]
                prevv = dma_counts.get(id(sem), 0)
                if prevv:
                    op.pre = (sem, prevv)
                op.sem = sem
                op.val = prevv + 16
                dma_counts[id(sem)] = op.val
            elif op.marked:
                cnt = counts["e_" + op.eng]
                if op.eng in CHAIN and cnt > 0:
                    op.pre = (sems["e_" + op.eng], cnt)
                counts["e_" + op.eng] = cnt + 1
                op.sem = sems["e_" + op.eng]
                op.val = cnt + 1
            elif op.eng in CHAIN:
                pass
        self.sems = sems
        self.drain_waits = []
        seen = set()
        for op in self.ops:
            if op.dma is not None and id(op.sem) not in seen:
                seen.add(id(op.sem))
        finals = {}
        for op in self.ops:
            if op.dma is not None:
                finals[id(op.sem)] = (op.sem, op.val)
        self.drain_waits = list(finals.values())

    def emit(self, eng_name, eng):
        watermark = {}
        for op in self.ops:
            if op.eng != eng_name:
                continue
            pend = list(op.waits)
            for w in pend:
                key = id(w.sem)
                if watermark.get(key, -1) >= w.val:
                    continue
                eng.wait_ge(w.sem, w.val)
                watermark[key] = w.val
            if op.pre is not None:
                sem, v = op.pre
                if watermark.get(id(sem), -1) < v:
                    eng.wait_ge(sem, v)
                    watermark[id(sem)] = v
            inst = op.fn(eng)
            if op.marked:
                inst.then_inc(op.sem, 16 if op.dma is not None else 1)
        if eng_name == "sync":
            for sem, v in self.drain_waits:
                if watermark.get(id(sem), -1) < v:
                    eng.wait_ge(sem, v)


def build_program(dbg=()):
    nc = bass.Bass("TRN2", target_bir_lowering=False)
    dt = nc.dram_tensor
    E = {}

    for n, s in (("XT", [D, T]), ("WqT", [128, 16384]), ("WkT", [128, 16384]),
                 ("WvT", [128, 16384]), ("WoT", [128, 16384]),
                 ("lq1", [H, 128]), ("lk1", [H, 128]), ("lq2", [H, 128]), ("lk2", [H, 128]),
                 ("rms_scale", [1, DV]), ("rw", [128, 128]),
                 ("gw", [NE, 128, 4096]), ("uw", [NE, 128, 4096]), ("dw", [NE, 128, 4096]),
                 ("sgw", [128, 8192]), ("suw", [128, 8192]), ("sdw", [128, 8192]),
                 ("identf", [128, 128]), ("onesf", [1, 128])):
        E[n] = dt(n, s, F32, kind="ExternalInput").ap()
    E["self"] = dt("self", [NE, NE * 128], F32, kind="ExternalInput").ap()
    E["YT"] = dt("YT", [D, T], F32, kind="ExternalOutput").ap()

    def scratch(name, shape, dtype):
        kind = "ExternalOutput" if name in dbg else "Internal"
        return dt(name, shape, dtype, kind=kind).ap()

    if "probe_u0" in dbg:
        E["DBG_P"] = dt("DBG_P", [128, 4096], BF16, kind="ExternalOutput").ap()
        E["DBG_E2"] = dt("DBG_E2", [128, 4096], BF16, kind="ExternalOutput").ap()
        E["DBG_DEN"] = dt("DBG_DEN", [128, 32], F32, kind="ExternalOutput").ap()
        E["DBG_PT"] = dt("DBG_PT", [128, 4096], BF16, kind="ExternalOutput").ap()
        E["DBG_ON"] = dt("DBG_ON", [128, 256], F32, kind="ExternalOutput").ap()
        E["DBG_OTM"] = dt("DBG_OTM", [128, 256], BF16, kind="ExternalOutput").ap()
        E["DBG_Q"] = dt("DBG_Q", [128, 8192], BF16, kind="ExternalOutput").ap()
        E["DBG_K"] = dt("DBG_K", [128, 8192], BF16, kind="ExternalOutput").ap()
        E["DBG_V"] = dt("DBG_V", [128, 8192], BF16, kind="ExternalOutput").ap()
    E["QTh"] = scratch("QTh", [2048, T], BF16)
    E["KTh"] = scratch("KTh", [2048, T], BF16)
    E["VH"] = scratch("VH", [T, 2048], BF16)
    E["OT"] = scratch("OT", [2048, T], BF16)
    E["X1F"] = scratch("X1F", [D, T], F32)
    E["X1B"] = scratch("X1B", [D, T], BF16)

    from contextlib import ExitStack
    with ExitStack() as ctx:
        E["AB"] = ctx.enter_context(nc.sbuf_tensor("AB", [128, 73728], BF16))
        E["AFr"] = ctx.enter_context(nc.sbuf_tensor("AFr", [128, 15488], F32))
        E["S1p"] = ctx.enter_context(nc.psum_tensor("S1p", [128, 1024], F32))
        E["S2p"] = ctx.enter_context(nc.psum_tensor("S2p", [128, 1024], F32))
        E["trp0"] = ctx.enter_context(nc.psum_tensor("trp0", [128, 1024], BF16))
        E["trp1"] = ctx.enter_context(nc.psum_tensor("trp1", [128, 1024], BF16))
        E["ovp"] = ctx.enter_context(nc.psum_tensor("ovp", [128, 512], F32))
        E["pob"] = ctx.enter_context(nc.psum_tensor("pob", [128, 1024], BF16))

        p = Prog()
        emit_all(p, E, dbg)
        p.resolve(nc, ctx)

        with nc.Block() as block:
            @block.tensor
            def _(e):
                p.emit("tensor", e)

            @block.scalar
            def _(e):
                p.emit("scalar", e)

            @block.vector
            def _(e):
                p.emit("vector", e)

            @block.gpsimd
            def _(e):
                p.emit("gpsimd", e)

            @block.sync
            def _(e):
                p.emit("sync", e)
    return nc


def emit_all(p, E, dbg):
    AB, AFr = E["AB"], E["AFr"]
    S1p, S2p, trp0, trp1, ovp, pob = E["S1p"], E["S2p"], E["trp0"], E["trp1"], E["ovp"], E["pob"]

    def ab(off, n, p0=0, np_=128):
        return AB[p0:p0 + np_, off:off + n]

    def af(off, n, p0=0, np_=128):
        return AFr[p0:p0 + np_, off:off + n]

    fIDENT, fONES, fRMSB, fLAM = 0, 128, 256, 512
    fBIG = 640
    bIDENT, bONES, bSEL = 0, 128, 256
    bBIG = bSEL + 2048

    ld_id = p.add("sync", lambda e: e.dma_start(out=af(fIDENT, 128), in_=E["identf"]), dma="win")
    ld_on = p.add("sync", lambda e: e.dma_start(out=af(fONES, 128, 0, 1), in_=E["onesf"]), dma="win")
    ld_idb = p.add("gpsimd", lambda e: e.dma_start(out=ab(bIDENT, 128), in_=E["identf"]), dma="wing")
    ld_sel = p.add("gpsimd", lambda e: e.dma_start(out=ab(bSEL, 2048, 0, NE), in_=E["self"]), dma="wing")
    ms_ones = p.add("gpsimd", lambda e: e.memset(ab(bONES, 128), 1.0))

    fSM = fBIG
    ld_rs = p.add("sync", lambda e: e.dma_start(out=af(fSM, DV, 0, 1), in_=E["rms_scale"]), dma="win")
    mm_rs = p.add("tensor", lambda e: e.matmul(ovp[:, 0:256], af(fONES, 128, 0, 1), af(fSM, DV, 0, 1), start=True, stop=True), waits=[ld_on, ld_rs])
    ev_rs = p.add("vector", lambda e: e.tensor_scalar(af(fRMSB, DV), ovp[:, 0:256], 1.0 - LAMBDA_INIT, None, ALU.mult), waits=[mm_rs])

    fL = fBIG + 256
    fLs = fL + 512
    ldl = []
    for i, t in enumerate(("lq1", "lk1", "lq2", "lk2")):
        ldl.append(p.add("sync", lambda e, t=t, i=i: e.dma_start(out=af(fL + 128 * i, 128, 0, 8), in_=E[t]), dma="win"))
    p.add("vector", lambda e: e.tensor_tensor(af(fL, 128, 0, 8), af(fL, 128, 0, 8), af(fL + 128, 128, 0, 8), ALU.mult), waits=ldl[:2])
    p.add("vector", lambda e: e.tensor_tensor(af(fL + 256, 128, 0, 8), af(fL + 256, 128, 0, 8), af(fL + 384, 128, 0, 8), ALU.mult), waits=ldl[2:])
    rs1 = p.add("vector", lambda e: e.tensor_reduce(af(fLs, 1, 0, 8), af(fL, 128, 0, 8), AX.X, ALU.add))
    rs2 = p.add("vector", lambda e: e.tensor_reduce(af(fLs + 1, 1, 0, 8), af(fL + 256, 128, 0, 8), AX.X, ALU.add))
    ex12 = p.add("scalar", lambda e: e.activation(af(fLs + 2, 2, 0, 8), af(fLs, 2, 0, 8), AF.Exp), waits=[rs1, rs2])
    p.add("vector", lambda e: e.tensor_tensor(af(fLs + 4, 1, 0, 8), af(fLs + 3, 1, 0, 8), af(fLs + 2, 1, 0, 8), ALU.subtract), waits=[ex12])
    nl2 = p.add("vector", lambda e: e.tensor_scalar(af(fLs + 5, 1, 0, 8), af(fLs + 4, 1, 0, 8), -LAMBDA_INIT, None, ALU.add))
    trl = p.add("tensor", lambda e: e.transpose(ovp[0:1, 256:264], af(fLs + 5, 1, 0, 8), af(fIDENT, 8, 0, 8)), waits=[nl2, ev_rs, ld_id])
    evl = p.add("vector", lambda e: e.tensor_copy(af(fLs + 6, 8, 0, 1), ovp[0:1, 256:264]), waits=[trl])
    mml = p.add("tensor", lambda e: e.matmul(ovp[:, 264:272], af(fONES, 128, 0, 1), af(fLs + 6, 8, 0, 1), start=True, stop=True), waits=[evl])
    nlam = p.add("vector", lambda e: e.tensor_copy(af(fLAM, 8), ovp[:, 264:272]), waits=[mml])

    bXTB = bBIG
    bW = bXTB + 32768
    bW2 = bW + 16384
    bSTG = bW2 + 16384
    assert bSTG + 4096 <= 73728

    xchunks = [p.add("gpsimd",
                     lambda e, c=c: e.dma_start(out=ab(bXTB + 4096 * c, 4096), in_=E["XT"][c * 128:(c + 1) * 128, :]),
                     dma="wing") for c in range(8)]
    ldwq = p.add("gpsimd", lambda e: e.dma_start(out=ab(bW, 16384), in_=E["WqT"]), dma="wing")
    ldwk = p.add("gpsimd", lambda e: e.dma_start(out=ab(bW2, 16384), in_=E["WkT"]), dma="wing")

    vh_r = E["VH"].rearrange("(t p) o -> p t o", p=128)

    def proj_fm(dst, woff, wait_w, tag, stg_prev=None, ev_seed=None):
        dmas = []
        prev_stg = stg_prev
        prev_ev = dict(ev_seed) if ev_seed else {}
        for m in range(16):
            evs = []
            for n in range(8):
                bank = (m * 8 + n) % 2
                psl = (S1p[:, 0:512], S1p[:, 512:1024])[bank]
                mms = []
                for c in range(8):
                    w = []
                    if c == 0:
                        w = [wait_w] + xchunks
                        if prev_ev.get(bank) is not None:
                            w.append(prev_ev[bank])
                    mms.append(p.add(
                        "tensor",
                        lambda e, m=m, n=n, c=c, psl=psl, woff=woff: e.matmul(
                            psl, ab(woff + 2048 * c + 128 * m, 128),
                            ab(bXTB + 4096 * c + 512 * n, 512),
                            start=(c == 0), stop=(c == 7)),
                        waits=w, tag=f"{tag}mm"))
                w2 = [mms[-1]]
                if n == 0:
                    if dmas:
                        w2.append(dmas[-1])
                    elif prev_stg is not None:
                        w2.append(prev_stg)
                ev = p.add("scalar", lambda e, n=n, psl=psl: e.copy(ab(bSTG + 512 * n, 512), psl), waits=w2)
                prev_ev[bank] = ev
                evs.append(ev)
            dmas.append(p.add(
                "sync",
                lambda e, m=m, dst=dst: e.dma_start(out=dst[m * 128:(m + 1) * 128, :], in_=ab(bSTG, 4096)),
                waits=evs, dma="out", tag=f"{tag}dma"))
        return dmas, prev_ev

    qdmas, qev = proj_fm(E["QTh"], bW, ldwq, "q")
    ldwv = p.add("gpsimd", lambda e: e.dma_start(out=ab(bW, 16384), in_=E["WvT"]), waits=[qdmas[-1]], dma="wing")
    kdmas, _ = proj_fm(E["KTh"], bW2, ldwk, "k", stg_prev=qdmas[-1], ev_seed=qev)

    vdmas = []
    prev_ev = {}
    for m in range(32):
        evs = []
        for gi in range(4):
            bank = (m * 4 + gi) % 2
            psl = (S2p[:, 0:512], S2p[:, 512:1024])[bank]
            mms = []
            for c in range(8):
                w = []
                if c == 0:
                    w = [ldwv]
                    if prev_ev.get(bank) is not None:
                        w.append(prev_ev[bank])
                mms.append(p.add(
                    "tensor",
                    lambda e, m=m, gi=gi, c=c, psl=psl: e.matmul(
                        psl, ab(bXTB + 4096 * c + 128 * m, 128),
                        ab(bW + 2048 * c + 512 * gi, 512),
                        start=(c == 0), stop=(c == 7)),
                    waits=w, tag="vmm"))
            w2 = [mms[-1]]
            if gi == 0:
                w2.append(vdmas[-1] if vdmas else kdmas[-1])
            ev = p.add("scalar", lambda e, gi=gi, psl=psl: e.copy(ab(bSTG + 512 * gi, 512), psl), waits=w2)
            prev_ev[bank] = ev
            evs.append(ev)
        vdmas.append(p.add(
            "sync",
            lambda e, m=m: e.dma_start(out=vh_r[:, m, :], in_=ab(bSTG, 2048)),
            waits=evs, dma="out"))

    if "stop_a0" in dbg:
        p.add("sync", lambda e: e.nop(), waits=qdmas + kdmas + vdmas)
        return

    bQ = bBIG
    bK = [bBIG + 8192, bBIG + 16384]
    bV = [bBIG + 24576, bBIG + 32768]
    bE1 = [bBIG + 40960, bBIG + 45056, bBIG + 49152]
    bE2 = [bBIG + 53248, bBIG + 57344]
    bPT = bBIG + 61440
    bOST = bPT + 4096
    bOTM = bOST + 1024
    assert bOTM + 512 <= 73728

    fD1 = [fBIG + 1024, fBIG + 1040]
    fD2 = [fBIG + 1028, fBIG + 1044]
    fDEN = [fBIG + 1056, fBIG + 1056 + 16]
    fON = [fBIG + 1100, fBIG + 1100 + 256]
    fSQ = fBIG + 1100 + 512

    qt_r = E["QTh"].rearrange("(c p) t -> p c t", p=128)
    kt_r = E["KTh"].rearrange("(c p) t -> p c t", p=128)

    def load_head(h, buf, waits):
        lq = p.add("sync", lambda e, h=h: e.dma_start(
            out=AB[:, bQ:bQ + 8192].rearrange("p (c t) -> p c t", c=2),
            in_=qt_r[:, 2 * h:2 * h + 2, :]), dma="win",
            waits=waits + (qdmas if h == 0 else []))
        lk = p.add("sync", lambda e, h=h, buf=buf: e.dma_start(
            out=AB[:, bK[buf]:bK[buf] + 8192].rearrange("p (c t) -> p c t", c=2),
            in_=kt_r[:, 2 * h:2 * h + 2, :]), dma="win",
            waits=(kdmas if h == 0 else []))
        lv = p.add("sync", lambda e, h=h, buf=buf: e.dma_start(
            out=AB[:, bV[buf]:bV[buf] + 8192].rearrange("p (t o) -> p t o", t=32),
            in_=vh_r[:, :, 256 * h:256 * h + 256]), dma="win",
            waits=(vdmas if h == 0 else []))
        return [lq, lk, lv]

    NU = H * 32
    st = {
        "loads": {0: load_head(0, 0, [])},
        "exp_last": [None, None],
        "comb": {}, "exps": {}, "dens": {}, "tr_last": {}, "pv": {},
        "epi": {}, "sq": {}, "otm": {}, "otr": {}, "ose": {}, "otdma": {},
        "head_last_exp": {},
        "trev_last": [None, None],
    }
    ot_dmas = []

    def emit_head_prefetch(u):
        h, qt = u // 32, u % 32
        if qt == 0 and h >= 1:
            lq = p.add("sync", lambda e, h=h: e.dma_start(
                out=AB[:, bQ:bQ + 8192].rearrange("p (c t) -> p c t", c=2),
                in_=qt_r[:, 2 * h:2 * h + 2, :]), dma="win",
                waits=[st["head_last_exp"][h - 1]])
            st["loads"][h].append(lq)
        if qt == 2 and h + 1 < H:
            w = []
            if h >= 1:
                w.append(st["pv"][(h - 1) * 32 + 31])
            lk = p.add("sync", lambda e, h1=h + 1, buf=1 - (h % 2): e.dma_start(
                out=AB[:, bK[buf]:bK[buf] + 8192].rearrange("p (c t) -> p c t", c=2),
                in_=kt_r[:, 2 * h1:2 * h1 + 2, :]), dma="win",
                waits=w)
            lv = p.add("sync", lambda e, h1=h + 1, buf=1 - (h % 2): e.dma_start(
                out=AB[:, bV[buf]:bV[buf] + 8192].rearrange("p (t o) -> p t o", t=32),
                in_=vh_r[:, :, 256 * h1:256 * h1 + 256]), dma="win")
            st["loads"][h + 1] = [lk, lv]


    def emit_S_span(u, j):
        h, qt = u // 32, u % 32
        buf = h % 2
        sl = u % 2
        loads = st["loads"][h]
        oQ, oK = bQ, bK[buf]
        exps = st["exps"].setdefault(u, {1: [], 2: []})
        for br, Sp, qoff, koff, bEo, fDx in (
                (1, S1p, oQ, oK, bE1[u % 3], fD1[sl]),
                (2, S2p, oQ + 4096, oK + 4096, bE2[sl], fD2[sl])):
            mms = []
            for half in range(2):
                kb = 2 * j + half
                w = []
                if kb == 0:
                    w = list(loads)
                    if st["exp_last"][br - 1] is not None:
                        w.append(st["exp_last"][br - 1])
                elif half == 0:
                    w.append(exps[br][j - 1])
                mms.append(p.add(
                    "tensor",
                    lambda e, qt=qt, kb=kb, Sp=Sp, qoff=qoff, koff=koff, half=half: e.matmul(
                        Sp[:, 512 * half:512 * half + 512],
                        ab(qoff + 128 * qt, 128), ab(koff + 512 * kb, 512),
                        start=True, stop=True),
                    waits=w, tag=f"s{br}u{u}"))
            ew = list(mms)
            if j == 0:
                if br == 1:
                    if (u - 3) in st["tr_last"]:
                        ew.append(st["tr_last"][u - 3])
                    if (u - 2) in st["dens"]:
                        ew.append(st["dens"][u - 2][0])
                if br == 2 and (u - 2) in st["comb"]:
                    ew.append(st["comb"][u - 2])
            exps[br].append(p.add(
                "scalar",
                lambda e, j=j, Sp=Sp, bEo=bEo, fDx=fDx: e.activation(
                    ab(bEo + 1024 * j, 1024), Sp[:, :], AF.Exp, scale=SCALE,
                    accum_out=af(fDx + j, 1)),
                waits=ew, tag=f"e{br}u{u}"))
            if j == 3:
                st["exp_last"][br - 1] = exps[br][-1]
                if qt == 31 and br == 2:
                    st["head_last_exp"][h] = exps[2][-1]

    def emit_TR(u, ktg_lo, ktg_hi):
        sl = u % 3
        c2 = st["comb"][u]
        for ktg in range(ktg_lo, ktg_hi + 1):
            dst = (trp0, trp1)[ktg % 2]
            trs = []
            for i in range(8):
                kt = 8 * ktg + i
                w = []
                if i == 0:
                    w = [c2, ld_idb]
                    if st["trev_last"][ktg % 2] is not None:
                        w.append(st["trev_last"][ktg % 2])
                trs.append(p.add(
                    "tensor",
                    lambda e, kt=kt, dst=dst, sl=sl: e.transpose(
                        dst[:, 128 * (kt % 8):128 * (kt % 8) + 128],
                        ab(bE1[sl] + 128 * kt, 128), ab(bIDENT, 128)),
                    waits=w, tag=f"tru{u}"))
            w2 = [trs[-1]]
            if (u - 1) in st["pv"]:
                w2.append(st["pv"][u - 1])
            tev = p.add(
                "vector",
                lambda e, ktg=ktg, dst=dst: e.tensor_copy(ab(bPT + 1024 * ktg, 1024), dst[:, :]),
                waits=w2, tag=f"trevu{u}")
            st["trev_last"][ktg % 2] = tev
            st.setdefault("trevs", {}).setdefault(u, []).append(tev)
            if ktg == 3:
                st["tr_last"][u] = trs[-1]

    def emit_PV(u, half):
        h = u // 32
        buf = h % 2
        sl = u % 2
        oV = bV[buf]
        trevs = st["trevs"][u]
        pvs = st.setdefault("pvs", {}).setdefault(u, [])
        for kt in range(16 * half, 16 * half + 16):
            w = []
            if kt % 8 == 0:
                w.append(trevs[kt // 8])
            if kt == 0:
                if (u - 1) in st["epi"]:
                    w.append(st["epi"][u - 1])
                if u == 0:
                    w += [nlam, ev_rs]
            pvs.append(p.add(
                "tensor",
                lambda e, kt=kt, sl=sl, oV=oV: e.matmul(
                    ovp[:, 256 * sl:256 * sl + 256],
                    ab(bPT + 128 * kt, 128), ab(oV + 256 * kt, 256),
                    start=(kt == 0), stop=(kt == 31)),
                waits=w, tag=f"pvu{u}"))
        if half == 1:
            st["pv"][u] = pvs[-1]

    def emit_epilogue(u):
        sl = u % 2
        fDx = fDEN[sl]
        fo = fON[sl]
        ms, r1 = st["dens"][u]
        ew = [st["pv"][u], r1]
        if (u - 2) in st["sq"]:
            ew.append(st["sq"][u - 2])
        epi = p.add("vector", lambda e, sl=sl, fo=fo, fDx=fDx: e.tensor_scalar(
            af(fo, 256), ovp[:, 256 * sl:256 * sl + 256], af(fDx + 4, 1), None, ALU.mult),
            waits=ew, tag=f"epiu{u}")
        st["epi"][u] = epi
        sqm = p.add("vector", lambda e, fo=fo: e.tensor_tensor(
            af(fSQ, 256), af(fo, 256), af(fo, 256), ALU.mult), tag=f"squ{u}")
        st["sq"][u] = sqm
        t2 = p.add("vector", lambda e, fDx=fDx: e.tensor_scalar(
            af(fSQ, 256), af(fSQ, 256), 1.0 / DV, EPS, ALU.mult, ALU.add,
            accum_out=af(fDx + 7, 1)), tag=f"t2u{u}")
        ln_ = p.add("scalar", lambda e, fDx=fDx: e.activation(af(fDx + 8, 1), af(fDx + 7, 1), AF.Ln), waits=[t2])
        rinv = p.add("scalar", lambda e, fDx=fDx: e.activation(af(fDx + 9, 1), af(fDx + 8, 1), AF.Exp, scale=-0.5), tag=f"rinvu{u}")
        p.add("vector", lambda e, fo=fo, fDx=fDx: e.tensor_scalar(af(fo, 256), af(fo, 256), af(fDx + 9, 1), None, ALU.mult), waits=[rinv])
        otm_w = [ev_rs]
        if (u - 2) in st["otr"]:
            otm_w.append(st["otr"][u - 2][-1])
        otm = p.add("vector", lambda e, sl=sl, fo=fo: e.tensor_tensor(
            ab(bOTM + 256 * sl, 256), af(fo, 256), af(fRMSB, 256), ALU.mult), waits=otm_w, tag=f"otmu{u}")
        st["otm"][u] = otm

    def emit_S_dve(u):
        h, qt = u // 32, u % 32
        sl = u % 2
        fDx = fDEN[sl]
        exps = st["exps"][u]
        p.add("vector", lambda e, sl=sl, fDx=fDx: e.tensor_reduce(
            af(fDx, 2), AFr[:, fD1[sl]:fD1[sl] + 8].rearrange("p (b x) -> p b x", b=2),
            AX.X, ALU.add), waits=exps[1] + exps[2])
        rd = p.add("vector", lambda e, fDx=fDx: e.reciprocal(af(fDx + 4, 2), af(fDx, 2)))
        p.add("vector", lambda e, fDx=fDx: e.tensor_tensor(af(fDx + 3, 1), af(fDx, 1), af(fDx + 5, 1), ALU.mult))
        ms = p.add("vector", lambda e, h=h, fDx=fDx: e.tensor_tensor(af(fDx + 6, 1), af(fDx + 3, 1), af(fLAM + h, 1), ALU.mult), waits=[nlam])
        st["dens"][u] = (ms, rd)
        c1 = p.add("vector", lambda e, sl=sl, fDx=fDx: e.tensor_scalar(ab(bE2[sl], 4096), ab(bE2[sl], 4096), af(fDx + 6, 1), None, ALU.mult), tag=f"c1u{u}")
        c2 = p.add("gpsimd", lambda e, sl=sl, s3=u % 3: e.tensor_tensor(ab(bE1[s3], 4096), ab(bE1[s3], 4096), ab(bE2[sl], 4096), ALU.add),
                   waits=[c1, exps[1][-1]], tag=f"c2u{u}")
        st["comb"][u] = c2

    def stage_OTR(u):
        h, qt = u // 32, u % 32
        sl = u % 2
        otrs = []
        for chv in range(2):
            w = [st["otm"][u]]
            if qt % 4 == 0 and chv == 0 and (u - 4) in st["ose"]:
                w += st["ose"][u - 4]
            otrs.append(p.add(
                "tensor",
                lambda e, qt=qt, chv=chv, sl=sl: e.transpose(
                    pob[:, 256 * (qt % 4) + 128 * chv:256 * (qt % 4) + 128 * chv + 128],
                    ab(bOTM + 256 * sl + 128 * chv, 128), ab(bIDENT, 128)),
                waits=w, tag=f"otru{u}"))
        st["otr"][u] = otrs
        if qt % 4 == 3:
            qb = u - 3
            grp = (qt - 3) // 4
            pob_v = pob[:, :].rearrange("p (q x) -> p q x", q=4)
            oses = []
            all_otrs = [o for i in range(4) for o in st["otr"][u - 3 + i]]
            for ch in range(2):
                w = list(all_otrs)
                prevd = st["otdma"].get((h, grp - 1), st["otdma"].get((h - 1, 7)))
                if prevd is not None:
                    w.append(prevd[ch])
                oses.append(p.add(
                    "vector",
                    lambda e, ch=ch: e.tensor_copy(
                        AB[:, bOST + 512 * ch:bOST + 512 * ch + 512].rearrange("p (q x) -> p q x", q=4),
                        pob_v[:, :, 128 * ch:128 * ch + 128]),
                    waits=w, tag=f"oseu{u}"))
            st["ose"][qb] = oses
            pair = []
            for ch in range(2):
                dma = p.add(
                    "sync",
                    lambda e, h=h, grp=grp, ch=ch: e.dma_start(
                        out=E["OT"][h * 256 + ch * 128:h * 256 + ch * 128 + 128, grp * 512:grp * 512 + 512],
                        in_=ab(bOST + 512 * ch, 512)),
                    waits=[oses[ch]], dma="out", tag=f"otdma{h}.{grp}")
                ot_dmas.append(dma)
                pair.append(dma)
            st["otdma"][(h, grp)] = pair

    nheads = H
    for d in dbg:
        if isinstance(d, str) and d.startswith("nh"):
            nheads = int(d[2:])
    NU = nheads * 32

    for u in range(NU + 4):
        cur = u < NU
        prv = 2 <= u <= NU + 1
        if cur:
            emit_head_prefetch(u)
            emit_S_span(u, 0)
        if 4 <= u <= NU + 3:
            stage_OTR(u - 4)
        if prv:
            emit_TR(u - 2, 0, 0)
        if cur:
            emit_S_span(u, 1)
        if prv:
            emit_TR(u - 2, 1, 2)
        if 1 <= u <= NU:
            emit_S_dve(u - 1)
        if cur:
            emit_S_span(u, 2)
        if prv:
            emit_TR(u - 2, 3, 3)
            emit_PV(u - 2, 0)
        if cur:
            emit_S_span(u, 3)
        if prv:
            emit_PV(u - 2, 1)
            emit_epilogue(u - 2)

    if "stop_a1" in dbg:
        p.add("sync", lambda e: e.nop(), waits=ot_dmas)
        return

    bWO = bBIG
    bOB = [bWO + 16384, bWO + 24576]
    bRB = bWO + 32768
    bRSQ = bRB + 4096
    bX1B = bRSQ + 4096
    assert bX1B + 4096 <= 73728
    fXT = fBIG + 1024
    fR = fXT + 4096
    fX1 = fR + 4096
    fT = fX1 + 4096
    assert fT + 1536 <= 15488

    ldwo = p.add("gpsimd", lambda e: e.dma_start(out=ab(bWO, 16384), in_=E["WoT"]),
        dma="wing", waits=[st["head_last_exp"][H - 1], st["pv"][NU - 1]])
    ot_r = E["OT"].rearrange("(c p) t -> p c t", p=128)
    xt_r = E["XT"].rearrange("(c p) t -> p c t", p=128)
    x1f_r = E["X1F"].rearrange("(c p) t -> p c t", p=128)
    x1b_r = E["X1B"].rearrange("(c p) t -> p c t", p=128)

    prev = {}
    x1_dmas = []
    for n in range(8):
        bufn = n % 2
        lob_w = list(ot_dmas) if n == 0 else ([prev[("womm", n - 2)]] if ("womm", n - 2) in prev else [])
        lob = p.add("sync", lambda e, n=n, bufn=bufn: e.dma_start(
            out=AB[:, bOB[bufn]:bOB[bufn] + 8192].rearrange("p (c t) -> p c t", c=16),
            in_=ot_r[:, :, 512 * n:512 * n + 512]), dma="win", waits=lob_w)
        lxt = p.add("sync", lambda e, n=n: e.dma_start(
            out=AFr[:, fXT:fXT + 4096].rearrange("p (c t) -> p c t", c=8),
            in_=xt_r[:, :, 512 * n:512 * n + 512]), dma="win",
            waits=[prev[("rev_last", n - 1)]] if ("rev_last", n - 1) in prev else [])
        revs = []
        for m in range(8):
            psl = (S1p[:, 0:512], S1p[:, 512:1024])[m % 2]
            mms = []
            for ec in range(16):
                w = []
                if ec == 0:
                    w = [lob, ldwo]
                    if ("rev", n, m - 2) in prev:
                        w.append(prev[("rev", n, m - 2)])
                    if m < 2 and ("rev", n - 1, 6 + m) in prev:
                        w.append(prev[("rev", n - 1, 6 + m)])
                mms.append(p.add(
                    "tensor",
                    lambda e, m=m, ec=ec, psl=psl, bufn=bufn: e.matmul(
                        psl, ab(bWO + 1024 * ec + 128 * m, 128), ab(bOB[bufn] + 512 * ec, 512),
                        start=(ec == 0), stop=(ec == 15)),
                    waits=w, tag=f"wo{n}.{m}"))
            prev[("womm", n)] = mms[-1]
            rev = p.add("vector", lambda e, m=m, psl=psl: e.tensor_tensor(
                af(fR + 512 * m, 512), psl, af(fXT + 512 * m, 512), ALU.add),
                waits=[mms[-1], lxt], tag=f"rev{n}.{m}")
            prev[("rev", n, m)] = rev
            revs.append(rev)
        rbfs, rsqs = [], []
        for m in range(8):
            rbfs.append(p.add("scalar", lambda e, m=m: e.copy(ab(bRB + 512 * m, 512), af(fR + 512 * m, 512)), waits=[revs[m]]))
            rsqs.append(p.add("scalar", lambda e, m=m: e.activation(ab(bRSQ + 512 * m, 512), af(fR + 512 * m, 512), AF.Square), waits=[revs[m]]))
        summ, sqmm = [], []
        for c in range(8):
            w1 = [rbfs[c], ms_ones]
            w2 = [rsqs[c]]
            if c == 0 and ("stat", n - 1) in prev:
                w1.append(prev[("stat", n - 1)])
                w2.append(prev[("stat", n - 1)])
            summ.append(p.add("tensor", lambda e, c=c: e.matmul(
                S2p[:, 0:512], ab(bONES, 128), ab(bRB + 512 * c, 512), start=(c == 0), stop=(c == 7)), waits=w1))
            sqmm.append(p.add("tensor", lambda e, c=c: e.matmul(
                S2p[:, 512:1024], ab(bONES, 128), ab(bRSQ + 512 * c, 512), start=(c == 0), stop=(c == 7)), waits=w2))
        mb = p.add("vector", lambda e: e.tensor_scalar(af(fT, 512), S2p[:, 0:512], 1.0 / D, None, ALU.mult), waits=[summ[-1]])
        t = p.add("vector", lambda e: e.tensor_scalar(af(fT + 512, 512), S2p[:, 512:1024], 1.0 / D, EPS, ALU.mult, ALU.add), waits=[sqmm[-1]])
        prev[("stat", n)] = t
        p.add("vector", lambda e: e.tensor_tensor(af(fT + 1024, 512), af(fT, 512), af(fT, 512), ALU.mult))
        var = p.add("vector", lambda e: e.tensor_tensor(af(fT + 1024, 512), af(fT + 512, 512), af(fT + 1024, 512), ALU.subtract))
        lnv = p.add("scalar", lambda e: e.activation(af(fT + 512, 512), af(fT + 1024, 512), AF.Ln), waits=[var])
        rstd = p.add("scalar", lambda e: e.activation(af(fT + 1024, 512), af(fT + 512, 512), AF.Exp, scale=-0.5), waits=[lnv])
        x1f_ops, x1b_ops = [], []
        for m in range(8):
            wv = [rstd]
            if ("x1dma", n - 1) in prev and m == 0:
                wv.append(prev[("x1dma", n - 1)])
            p.add("vector", lambda e, m=m: e.tensor_tensor(af(fX1 + 512 * m, 512), af(fR + 512 * m, 512), af(fT, 512), ALU.subtract), waits=wv)
            x1c = p.add("vector", lambda e, m=m: e.tensor_tensor(af(fX1 + 512 * m, 512), af(fX1 + 512 * m, 512), af(fT + 1024, 512), ALU.mult))
            wb = [x1c]
            if m == 0 and ("x1dmab", n - 1) in prev:
                wb.append(prev[("x1dmab", n - 1)])
            x1b = p.add("scalar", lambda e, m=m: e.copy(ab(bX1B + 512 * m, 512), af(fX1 + 512 * m, 512)), waits=wb)
            x1f_ops.append(x1c)
            x1b_ops.append(x1b)
        rl = p.add("vector", lambda e: e.tensor_copy(af(fLAM, 1), af(fLAM, 1)))
        prev[("rev_last", n)] = rl
        d1 = p.add("sync", lambda e, n=n: e.dma_start(
            out=x1f_r[:, :, 512 * n:512 * n + 512],
            in_=AFr[:, fX1:fX1 + 4096].rearrange("p (c t) -> p c t", c=8)),
            waits=x1f_ops, dma="out")
        d2 = p.add("sync", lambda e, n=n: e.dma_start(
            out=x1b_r[:, :, 512 * n:512 * n + 512],
            in_=AB[:, bX1B:bX1B + 4096].rearrange("p (c t) -> p c t", c=8)),
            waits=x1b_ops, dma="out")
        prev[("x1dma", n)] = d1
        prev[("x1dmab", n)] = d2
        x1_dmas += [d1, d2]

    if "stop_b" in dbg:
        p.add("sync", lambda e: e.nop(), waits=x1_dmas)
        return

    bXB = bBIG
    bRACC = bXB + 8192
    bINT = bRACC + 8192
    bWG = [bINT + 8192, bINT + 20480, bINT + 32768]
    bG = bINT + 45056
    bCT = bG + 512
    bRW = bCT + 1024
    bCF = bRW + 128
    bLNB = bINT
    assert bCF + 16 <= 73728
    fCB = fBIG + 1024
    fLG = fCB + 1024
    fLGT = fLG + 1024
    fPRB = fLGT + 128
    fCFT = fPRB + 112
    fX1Q = fCFT + 16
    fR2 = fX1Q + 4096
    fT2 = fR2 + 4096
    assert fT2 + 1536 <= 15488

    ldrw = p.add("gpsimd", lambda e: e.dma_start(out=ab(bRW, 128), in_=E["rw"]), dma="wing",
        waits=[prev[("womm", 7)]])

    y_dmas = []
    cp = {}
    for Q in range(4):
        tq = 1024 * Q
        lxb = p.add("sync", lambda e, tq=tq: e.dma_start(
            out=AB[:, bXB:bXB + 8192].rearrange("p (c t) -> p c t", c=8),
            in_=x1b_r[:, :, tq:tq + 1024]), dma="win",
            waits=([prev[("x1dmab", n) ] for n in range(8)] if Q == 0 else [cp[("edn_last",)]]))
        lgevs = []
        for tt in range(2):
            mms = []
            for c in range(8):
                w = []
                if c == 0:
                    w = [lxb, ldrw]
                    if tt == 1:
                        w.append(lgevs[0])
                    elif ("ovp_free",) in cp:
                        w.append(cp[("ovp_free",)])
                    elif Q == 0:
                        w.append(st["epi"].get(NU - 1))
                mms.append(p.add("tensor", lambda e, tt=tt, c=c: e.matmul(
                    ovp[0:16, 0:512], ab(bRW + 16 * c, 16, 0, 128), ab(bXB + 1024 * c + 512 * tt, 512),
                    start=(c == 0), stop=(c == 7)), waits=w))
            lgevs.append(p.add("vector", lambda e, tt=tt: e.tensor_copy(af(fLG + 512 * tt, 512, 0, 16), ovp[0:16, 0:512]), waits=[mms[-1]]))
        trls = []
        for j in range(8):
            w = [lgevs[-1]] if j == 0 else []
            trls.append(p.add("tensor", lambda e, j=j: e.transpose(
                ovp[:, 16 * j:16 * j + 16], af(fLG + 128 * j, 128, 0, 16), af(fIDENT, 16, 0, 16)), waits=w))
        lgt = p.add("vector", lambda e: e.tensor_copy(af(fLGT, 128), ovp[:, 0:128]), waits=[trls[-1]])
        ctevs = []
        last_ovp_reader = lgt
        for j in range(8):
            m1r = p.add("vector", lambda e, j=j: e.reduce_max(af(fPRB, 1), af(fLGT + 16 * j, 16), AX.X))
            nm1 = p.add("vector", lambda e: e.tensor_scalar(af(fPRB + 5, 1), af(fPRB, 1), -1.0, None, ALU.mult))
            ex = p.add("scalar", lambda e, j=j: e.activation(af(fPRB + 16, 16), af(fLGT + 16 * j, 16), AF.Exp, bias=af(fPRB + 5, 1), accum_out=af(fPRB + 1, 1)), waits=[nm1])
            p.add("vector", lambda e: e.reciprocal(af(fPRB + 2, 1), af(fPRB + 1, 1)), waits=[ex])
            p.add("vector", lambda e: e.tensor_scalar(af(fPRB + 32, 16), af(fPRB + 16, 16), af(fPRB + 2, 1), None, ALU.mult))
            p.add("vector", lambda e: e.reduce_max(af(fPRB + 3, 1), af(fPRB + 32, 16), AX.X))
            p.add("vector", lambda e: e.tensor_scalar(af(fPRB + 48, 16), af(fPRB + 32, 16), af(fPRB + 3, 1), None, ALU.is_equal))
            p.add("vector", lambda e: e.tensor_tensor(af(fPRB + 64, 16), af(fPRB + 32, 16), af(fPRB + 48, 16), ALU.mult))
            p.add("vector", lambda e: e.tensor_tensor(af(fPRB + 64, 16), af(fPRB + 32, 16), af(fPRB + 64, 16), ALU.subtract))
            p.add("vector", lambda e: e.reduce_max(af(fPRB + 4, 1), af(fPRB + 64, 16), AX.X))
            p.add("vector", lambda e: e.tensor_scalar(af(fPRB + 48, 16), af(fPRB + 32, 16), af(fPRB + 4, 1), None, ALU.is_ge))
            cf = p.add("vector", lambda e: e.tensor_tensor(af(fCFT, 16), af(fPRB + 32, 16), af(fPRB + 48, 16), ALU.mult))
            cfb = p.add("scalar", lambda e: e.copy(ab(bCF, 16), af(fCFT, 16)), waits=[cf])
            trc = p.add("tensor", lambda e, j=j: e.transpose(trp0[0:16, 0:128], ab(bCF, 16), ab(bIDENT, 128)), waits=[cfb, (ctevs[-1] if ctevs else None)])
            ctevs.append(p.add("vector", lambda e, j=j: e.tensor_copy(ab(bCT + 128 * j, 128, 0, NE), trp0[0:16, 0:128]), waits=[trc]))
        lsg = p.add("gpsimd", lambda e: e.dma_start(out=ab(bWG[0], 8192), in_=E["sgw"]), dma="wing",
            waits=([ldrw, prev[("x1dmab", 7)], prev[("x1dma", 7)]] if Q == 0 else [cp[("edn_last",)]]))
        lsu = p.add("gpsimd", lambda e: e.dma_start(out=ab(bWG[1], 8192), in_=E["suw"]), dma="wing")
        last_sil = cp.get(("lnb_pe_done",))
        last_mi = None
        gu_done = []
        for fm in range(8):
            for tt in range(2):
                gm = []
                for c in range(8):
                    w = []
                    if c == 0:
                        w = [lsg, lxb]
                        if last_sil is not None:
                            w.append(last_sil)
                    gm.append(p.add("tensor", lambda e, fm=fm, tt=tt, c=c: e.matmul(
                        S1p[:, 0:512], ab(bWG[0] + 1024 * c + 128 * fm, 128), ab(bXB + 1024 * c + 512 * tt, 512),
                        start=(c == 0), stop=(c == 7)), waits=w))
                sil = p.add("scalar", lambda e: e.activation(ab(bG + 16, 496) if False else ab(bG, 512), S1p[:, 0:512], AF.Silu), waits=[gm[-1], (last_mi if last_mi is not None else None)], tag=f"sil{Q}.{fm}.{tt}")
                last_sil = sil
                um = []
                for c in range(8):
                    w = []
                    if c == 0:
                        w = [lsu]
                        if last_mi is not None:
                            w.append(last_mi)
                    um.append(p.add("tensor", lambda e, fm=fm, tt=tt, c=c: e.matmul(
                        S1p[:, 512:1024], ab(bWG[1] + 1024 * c + 128 * fm, 128), ab(bXB + 1024 * c + 512 * tt, 512),
                        start=(c == 0), stop=(c == 7)), waits=w))
                mi = p.add("vector", lambda e, fm=fm, tt=tt: e.tensor_tensor(
                    ab(bINT + 1024 * fm + 512 * tt, 512), ab(bG, 512), S1p[:, 512:1024], ALU.mult),
                    waits=[um[-1], sil], tag=f"smi{Q}.{fm}.{tt}")
                last_mi = mi
                gu_done.append(mi)
        lsd = p.add("gpsimd", lambda e: e.dma_start(out=ab(bWG[0], 8192), in_=E["sdw"]), dma="wing", waits=[gu_done[-1], last_sil])
        last_rc = [cp.get(("lnb_pe_done",)), cp.get(("lnb_pe_done",))]
        dn_ops = []
        for dm in range(8):
            for tt in range(2):
                bank = tt
                psl = (S2p[:, 0:512], S2p[:, 512:1024])[bank]
                mm = []
                for c in range(8):
                    w = []
                    if c == 0:
                        w = [lsd] + gu_done[tt::2]
                        if last_rc[bank] is not None:
                            w.append(last_rc[bank])
                    mm.append(p.add("tensor", lambda e, dm=dm, tt=tt, c=c, psl=psl: e.matmul(
                        psl, ab(bWG[0] + 1024 * c + 128 * dm, 128), ab(bINT + 1024 * c + 512 * tt, 512),
                        start=(c == 0), stop=(c == 7)), waits=w))
                rc = p.add("vector", lambda e, dm=dm, tt=tt, psl=psl: e.tensor_copy(
                    ab(bRACC + 1024 * dm + 512 * tt, 512), psl), waits=[mm[-1]], tag=f"sracc{Q}.{dm}.{tt}")
                last_rc[bank] = rc
                dn_ops.append(rc)
        for ei in range(NE):
            wbuf = ei % 3
            if ei >= 3:
                wld_w = [cp[("emi_last", ei - 3)], cp[("edn_mm_last", ei - 3)]]
            elif ei == 0:
                wld_w = [dn_ops[-1]]
            elif ei == 1:
                wld_w = [gu_done[-1], dn_ops[-1]]
            else:
                wld_w = [w for w in (cp.get(("emi_last", 14)), cp.get(("edn_mm_last", 14))) if w is not None]
            lg_ = p.add("gpsimd", lambda e, ei=ei, wbuf=wbuf: e.dma_start(
                out=ab(bWG[wbuf], 4096), in_=E["gw"][ei]), dma="wing", waits=wld_w)
            lu_ = p.add("gpsimd", lambda e, ei=ei, wbuf=wbuf: e.dma_start(
                out=ab(bWG[wbuf] + 4096, 4096), in_=E["uw"][ei]), dma="wing")
            ld_ = p.add("gpsimd", lambda e, ei=ei, wbuf=wbuf: e.dma_start(
                out=ab(bWG[wbuf] + 8192, 4096), in_=E["dw"][ei]), dma="wing")
            cbevs = []
            for tt in range(2):
                w = [ld_sel, ctevs[-1], last_ovp_reader]
                cbm = p.add("tensor", lambda e, ei=ei, tt=tt: e.matmul(
                    ovp[:, 0:512], ab(bSEL + 128 * ei, 128, 0, NE), ab(bCT + 512 * tt, 512, 0, NE),
                    start=True, stop=True), waits=w)
                cbev = p.add("vector", lambda e, tt=tt: e.tensor_copy(af(fCB + 512 * tt, 512), ovp[:, 0:512]), waits=[cbm])
                cbevs.append(cbev)
                last_ovp_reader = cbev
            emi = []
            last_esil = cp.get(("esil_last",))
            last_emi = cp.get(("emi_bank_last",))
            for fm in range(4):
                for tt in range(2):
                    gm = []
                    for c in range(8):
                        w = []
                        if c == 0:
                            w = [lg_]
                            if ei == 0 and fm == 0 and tt == 0:
                                w.append(dn_ops[-1])
                                w.append(gu_done[-1])
                            if last_esil is not None:
                                w.append(last_esil)
                        gm.append(p.add("tensor", lambda e, fm=fm, tt=tt, c=c, wbuf=wbuf: e.matmul(
                            S1p[:, 0:512], ab(bWG[wbuf] + 512 * c + 128 * fm, 128), ab(bXB + 1024 * c + 512 * tt, 512),
                            start=(c == 0), stop=(c == 7)), waits=w))
                    sil = p.add("scalar", lambda e: e.activation(ab(bG, 512), S1p[:, 0:512], AF.Silu),
                                waits=[gm[-1], (last_emi if last_emi is not None else None)])
                    last_esil = sil
                    um = []
                    for c in range(8):
                        w = []
                        if c == 0:
                            w = [lu_]
                            if last_emi is not None:
                                w.append(last_emi)
                        um.append(p.add("tensor", lambda e, fm=fm, tt=tt, c=c, wbuf=wbuf: e.matmul(
                            S1p[:, 512:1024], ab(bWG[wbuf] + 4096 + 512 * c + 128 * fm, 128), ab(bXB + 1024 * c + 512 * tt, 512),
                            start=(c == 0), stop=(c == 7)), waits=w))
                    mi1 = p.add("vector", lambda e, fm=fm, tt=tt: e.tensor_tensor(
                        ab(bINT + 1024 * fm + 512 * tt, 512), ab(bG, 512), S1p[:, 512:1024], ALU.mult),
                        waits=[um[-1], sil])
                    mi2 = p.add("vector", lambda e, fm=fm, tt=tt: e.tensor_tensor(
                        ab(bINT + 1024 * fm + 512 * tt, 512), ab(bINT + 1024 * fm + 512 * tt, 512), af(fCB + 512 * tt, 512), ALU.mult),
                        waits=[cbevs[tt]], tag=f"emi{Q}.{ei}.{fm}.{tt}")
                    last_emi = mi1
                    emi.append(mi2)
            cp[("esil_last",)] = last_esil
            cp[("emi_bank_last",)] = last_emi
            cp[("emi_last", ei)] = emi[-1]
            edn = []
            for dm in range(8):
                for tt in range(2):
                    bank = tt
                    psl = (S2p[:, 0:512], S2p[:, 512:1024])[bank]
                    mm = []
                    for c in range(4):
                        w = []
                        if c == 0:
                            w = [ld_] + emi[tt::2]
                            if last_rc[bank] is not None:
                                w.append(last_rc[bank])
                        mm.append(p.add("tensor", lambda e, dm=dm, tt=tt, c=c, psl=psl, wbuf=wbuf: e.matmul(
                            psl, ab(bWG[wbuf] + 8192 + 1024 * c + 128 * dm, 128), ab(bINT + 1024 * c + 512 * tt, 512),
                            start=(c == 0), stop=(c == 3)), waits=w))
                    rc = p.add("vector", lambda e, dm=dm, tt=tt, psl=psl: e.tensor_tensor(
                        ab(bRACC + 1024 * dm + 512 * tt, 512), ab(bRACC + 1024 * dm + 512 * tt, 512), psl, ALU.add),
                        waits=[mm[-1]], tag=f"eracc{Q}.{ei}.{dm}.{tt}")
                    last_rc[bank] = rc
                    edn.append(rc)
            cp[("edn_mm_last", ei)] = mm[-1]
            cp[("edn_last",)] = edn[-1]
        cp[("ovp_free",)] = last_ovp_reader
        lnb_pe = None
        for tt in range(2):
            if Q == 0 and tt == 0:
                lx1_w = [prev[("x1dma", n)] for n in range(8)]
            elif tt == 1:
                lx1_w = [cp[("ydma", Q, 0)]]
            else:
                lx1_w = [cp[("ydma", Q - 1, 1)]]
            lx1 = p.add("sync", lambda e, tq=tq, tt=tt: e.dma_start(
                out=AFr[:, fX1Q:fX1Q + 4096].rearrange("p (c t) -> p c t", c=8),
                in_=x1f_r[:, :, tq + 512 * tt:tq + 512 * tt + 512]), dma="win", waits=lx1_w)
            r2s = []
            for m in range(8):
                r2s.append(p.add("vector", lambda e, m=m, tt=tt: e.tensor_tensor(
                    af(fR2 + 512 * m, 512), af(fX1Q + 512 * m, 512), ab(bRACC + 1024 * m + 512 * tt, 512), ALU.add),
                    waits=[lx1, cp[("edn_last",)]], tag=f"r2.{Q}.{tt}.{m}"))
            rbfs, rsqs = [], []
            for m in range(8):
                w = [r2s[m]]
                if tt == 0 and m == 0:
                    w.append(cp[("emi_last", NE - 1)])
                rbfs.append(p.add("scalar", lambda e, m=m: e.copy(ab(bLNB + 512 * m, 512), af(fR2 + 512 * m, 512)), waits=w))
                rsqs.append(p.add("scalar", lambda e, m=m: e.activation(ab(bLNB + 4096 + 512 * m, 512), af(fR2 + 512 * m, 512), AF.Square), waits=[r2s[m]]))
            summ, sqmm = [], []
            for c in range(8):
                w1 = [rbfs[c], ms_ones]
                w2 = [rsqs[c]]
                if c == 0:
                    w1.append(cp[("edn_mm_last", NE - 1)])
                    if ("lnstat", Q, tt - 1) in cp:
                        w1.append(cp[("lnstat", Q, tt - 1)])
                        w2.append(cp[("lnstat", Q, tt - 1)])
                    elif ("lnstat", Q - 1, 1) in cp:
                        w1.append(cp[("lnstat", Q - 1, 1)])
                        w2.append(cp[("lnstat", Q - 1, 1)])
                summ.append(p.add("tensor", lambda e, c=c: e.matmul(
                    S1p[:, 0:512], ab(bONES, 128), ab(bLNB + 512 * c, 512), start=(c == 0), stop=(c == 7)), waits=w1))
                sqmm.append(p.add("tensor", lambda e, c=c: e.matmul(
                    S1p[:, 512:1024], ab(bONES, 128), ab(bLNB + 4096 + 512 * c, 512), start=(c == 0), stop=(c == 7)), waits=w2))
            mb = p.add("vector", lambda e: e.tensor_scalar(af(fT2, 512), S1p[:, 0:512], 1.0 / D, None, ALU.mult), waits=[summ[-1]])
            t = p.add("vector", lambda e: e.tensor_scalar(af(fT2 + 512, 512), S1p[:, 512:1024], 1.0 / D, EPS, ALU.mult, ALU.add), waits=[sqmm[-1]])
            cp[("lnstat", Q, tt)] = t
            p.add("vector", lambda e: e.tensor_tensor(af(fT2 + 1024, 512), af(fT2, 512), af(fT2, 512), ALU.mult))
            var = p.add("vector", lambda e: e.tensor_tensor(af(fT2 + 1024, 512), af(fT2 + 512, 512), af(fT2 + 1024, 512), ALU.subtract))
            lnv = p.add("scalar", lambda e: e.activation(af(fT2 + 512, 512), af(fT2 + 1024, 512), AF.Ln), waits=[var])
            rstd = p.add("scalar", lambda e: e.activation(af(fT2 + 1024, 512), af(fT2 + 512, 512), AF.Exp, scale=-0.5), waits=[lnv])
            yevs = []
            for m in range(8):
                p.add("vector", lambda e, m=m: e.tensor_tensor(af(fX1Q + 512 * m, 512), af(fR2 + 512 * m, 512), af(fT2, 512), ALU.subtract), waits=[rstd])
                yevs.append(p.add("vector", lambda e, m=m: e.tensor_tensor(af(fX1Q + 512 * m, 512), af(fX1Q + 512 * m, 512), af(fT2 + 1024, 512), ALU.mult)))
            yd = p.add("sync", lambda e, tq=tq, tt=tt: e.dma_start(
                out=E["YT"].rearrange("(c p) t -> p c t", p=128)[:, :, tq + 512 * tt:tq + 512 * tt + 512],
                in_=AFr[:, fX1Q:fX1Q + 4096].rearrange("p (c t) -> p c t", c=8)),
                waits=yevs, dma="out")
            cp[("ydma", Q, tt)] = yd
            y_dmas.append(yd)
        cp[("lnb_pe_done",)] = cp[("lnstat", Q, 1)]

    p.add("sync", lambda e: e.nop(), waits=y_dmas)


def make_sel():
    s = np.zeros((NE, NE * 128), dtype=np.float32)
    for e in range(NE):
        s[e, e * 128:(e + 1) * 128] = 1.0
    return s


def swz(m):
    m = np.asarray(m, dtype=np.float32)
    c = m.shape[0] // 128
    return np.ascontiguousarray(m.reshape(c, 128, m.shape[1]).transpose(1, 0, 2).reshape(128, -1))


def make_inputs_for_core(inputs, b):
    f32 = lambda x: np.ascontiguousarray(np.asarray(x, dtype=np.float32))
    d = {}
    d["XT"] = f32(np.asarray(inputs["L0_attn_X"][b]).T)
    d["WqT"] = swz(np.asarray(inputs["L0_attn_Wq"]).T)
    d["WkT"] = swz(np.asarray(inputs["L0_attn_Wk"]).T)
    d["WvT"] = swz(np.asarray(inputs["L0_attn_Wv"]).T)
    d["WoT"] = swz(np.asarray(inputs["L0_attn_Wo"]).T)
    d["lq1"] = f32(inputs["L0_attn_lambda_q1"])
    d["lk1"] = f32(inputs["L0_attn_lambda_k1"])
    d["lq2"] = f32(inputs["L0_attn_lambda_q2"])
    d["lk2"] = f32(inputs["L0_attn_lambda_k2"])
    d["rms_scale"] = f32(np.asarray(inputs["L0_attn_rms_scale"])[None, :])
    d["rw"] = swz(np.asarray(inputs["L0_ffn_router_w"]))
    d["gw"] = np.stack([swz(np.asarray(inputs["L0_ffn_gate_w"])[e]) for e in range(NE)])
    d["uw"] = np.stack([swz(np.asarray(inputs["L0_ffn_up_w"])[e]) for e in range(NE)])
    d["dw"] = np.stack([swz(np.asarray(inputs["L0_ffn_down_w"])[e]) for e in range(NE)])
    d["sgw"] = swz(np.asarray(inputs["L0_ffn_shared_gate_w"]))
    d["suw"] = swz(np.asarray(inputs["L0_ffn_shared_up_w"]))
    d["sdw"] = swz(np.asarray(inputs["L0_ffn_shared_down_w"]))
    d["identf"] = np.eye(128, dtype=np.float32)
    d["onesf"] = np.ones((1, 128), dtype=np.float32)
    d["self"] = make_sel()
    return d


def kernel(**inputs):
    nc = build_program()
    in_maps = [make_inputs_for_core(inputs, b) for b in range(8)]
    res = run_bass_kernel_spmd(nc, in_maps, core_ids=list(range(8)))
    out = np.stack([np.asarray(r["YT"]).T for r in res.results], axis=0)
    return np.ascontiguousarray(out).astype(np.float32)


# version 12
# speedup vs baseline: 1.1716x; 1.1436x over previous
import math
import numpy as np
import ml_dtypes
import concourse.bass as bass
import concourse.mybir as mybir
from concourse.bass_utils import run_bass_kernel_spmd

F32 = mybir.dt.float32
BF16 = mybir.dt.bfloat16
AF = mybir.ActivationFunctionType
ALU = mybir.AluOpType
AX = mybir.AxisListType

T = 4096
D = 1024
H = 8
DV = 256
NE = 16
FE = 512
FS = 1024
SCALE = 1.0 / math.sqrt(128.0)
LAMBDA_INIT = 0.8 - 0.6 * math.exp(-0.3)
EPS = 1e-5

ENGINES = ("tensor", "scalar", "vector", "gpsimd", "sync")


DMA_POOL = 8


class Op:
    __slots__ = ("eng", "fn", "waits", "dma", "marked", "sem", "val", "pre", "tag")

    def __init__(self, eng, fn, waits, dma, tag):
        self.eng, self.fn, self.dma, self.tag = eng, fn, dma, tag
        self.waits = [w for w in waits if w is not None]
        self.marked = False
        self.sem = None
        self.val = None
        self.pre = None


class Prog:
    def __init__(self):
        self.ops = []

    def add(self, eng, fn, waits=(), dma=None, tag=""):
        op = Op(eng, fn, waits, dma, tag)
        self.ops.append(op)
        return op

    def resolve(self, nc, ctx):
        groups = sorted({op.dma for op in self.ops if op.dma is not None})
        sems = {}
        for e in ENGINES:
            sems["e_" + e] = ctx.enter_context(nc.semaphore("e_" + e))
        pool = {}
        for g in groups:
            pool[g] = [ctx.enter_context(nc.semaphore(f"d_{g}{i}")) for i in range(DMA_POOL)]
        CHAIN = ("vector", "scalar")
        for op in self.ops:
            kept = []
            for w in op.waits:
                if w.dma is None and w.eng == op.eng and op.eng not in CHAIN:
                    continue
                if w.dma is None and w.eng == op.eng and op.eng in CHAIN:
                    continue
                w.marked = True
                kept.append(w)
            op.waits = kept
        for op in self.ops:
            if op.eng in CHAIN and op.dma is None:
                op.marked = True
        counts = {("e_" + e): 0 for e in ENGINES}
        rr = {g: 0 for g in groups}
        dma_counts = {}
        for op in self.ops:
            if op.dma is not None:
                op.marked = True
                i = rr[op.dma]
                rr[op.dma] = (i + 1) % DMA_POOL
                sem = pool[op.dma][i]
                prevv = dma_counts.get(id(sem), 0)
                if prevv:
                    op.pre = (sem, prevv)
                op.sem = sem
                op.val = prevv + 16
                dma_counts[id(sem)] = op.val
            elif op.marked:
                cnt = counts["e_" + op.eng]
                if op.eng in CHAIN and cnt > 0:
                    op.pre = (sems["e_" + op.eng], cnt)
                counts["e_" + op.eng] = cnt + 1
                op.sem = sems["e_" + op.eng]
                op.val = cnt + 1
            elif op.eng in CHAIN:
                pass
        self.sems = sems
        self.drain_waits = []
        seen = set()
        for op in self.ops:
            if op.dma is not None and id(op.sem) not in seen:
                seen.add(id(op.sem))
        finals = {}
        for op in self.ops:
            if op.dma is not None:
                finals[id(op.sem)] = (op.sem, op.val)
        self.drain_waits = list(finals.values())

    def emit(self, eng_name, eng):
        watermark = {}
        for op in self.ops:
            if op.eng != eng_name:
                continue
            pend = list(op.waits)
            for w in pend:
                key = id(w.sem)
                if watermark.get(key, -1) >= w.val:
                    continue
                eng.wait_ge(w.sem, w.val)
                watermark[key] = w.val
            if op.pre is not None:
                sem, v = op.pre
                if watermark.get(id(sem), -1) < v:
                    eng.wait_ge(sem, v)
                    watermark[id(sem)] = v
            inst = op.fn(eng)
            if op.marked:
                inst.then_inc(op.sem, 16 if op.dma is not None else 1)
        if eng_name == "sync":
            for sem, v in self.drain_waits:
                if watermark.get(id(sem), -1) < v:
                    eng.wait_ge(sem, v)


def build_program(dbg=()):
    nc = bass.Bass("TRN2", target_bir_lowering=False)
    dt = nc.dram_tensor
    E = {}

    for n, s in (("XT", [D, T]), ("WqT", [128, 16384]), ("WkT", [128, 16384]),
                 ("WvT", [128, 16384]), ("WoT", [128, 16384]),
                 ("lq1", [H, 128]), ("lk1", [H, 128]), ("lq2", [H, 128]), ("lk2", [H, 128]),
                 ("rms_scale", [1, DV]), ("rw", [128, 128]),
                 ("gw", [NE, 128, 4096]), ("uw", [NE, 128, 4096]), ("dw", [NE, 128, 4096]),
                 ("sgw", [128, 8192]), ("suw", [128, 8192]), ("sdw", [128, 8192]),
                 ("identf", [128, 128]), ("onesf", [1, 128])):
        E[n] = dt(n, s, F32, kind="ExternalInput").ap()
    E["self"] = dt("self", [NE, NE * 128], F32, kind="ExternalInput").ap()
    E["YT"] = dt("YT", [D, T], F32, kind="ExternalOutput").ap()

    def scratch(name, shape, dtype):
        kind = "ExternalOutput" if name in dbg else "Internal"
        return dt(name, shape, dtype, kind=kind).ap()

    if "probe_u0" in dbg:
        E["DBG_P"] = dt("DBG_P", [128, 4096], BF16, kind="ExternalOutput").ap()
        E["DBG_E2"] = dt("DBG_E2", [128, 4096], BF16, kind="ExternalOutput").ap()
        E["DBG_DEN"] = dt("DBG_DEN", [128, 32], F32, kind="ExternalOutput").ap()
        E["DBG_PT"] = dt("DBG_PT", [128, 4096], BF16, kind="ExternalOutput").ap()
        E["DBG_ON"] = dt("DBG_ON", [128, 256], F32, kind="ExternalOutput").ap()
        E["DBG_OTM"] = dt("DBG_OTM", [128, 256], BF16, kind="ExternalOutput").ap()
        E["DBG_Q"] = dt("DBG_Q", [128, 8192], BF16, kind="ExternalOutput").ap()
        E["DBG_K"] = dt("DBG_K", [128, 8192], BF16, kind="ExternalOutput").ap()
        E["DBG_V"] = dt("DBG_V", [128, 8192], BF16, kind="ExternalOutput").ap()
    E["QTh"] = scratch("QTh", [2048, T], BF16)
    E["KTh"] = scratch("KTh", [2048, T], BF16)
    E["VH"] = scratch("VH", [T, 2048], BF16)
    E["OT"] = scratch("OT", [2048, T], BF16)
    E["X1F"] = scratch("X1F", [D, T], F32)
    E["X1B"] = scratch("X1B", [D, T], BF16)

    from contextlib import ExitStack
    with ExitStack() as ctx:
        E["AB"] = ctx.enter_context(nc.sbuf_tensor("AB", [128, 73728], BF16))
        E["AFr"] = ctx.enter_context(nc.sbuf_tensor("AFr", [128, 15488], F32))
        E["S1p"] = ctx.enter_context(nc.psum_tensor("S1p", [128, 1024], F32))
        E["S2p"] = ctx.enter_context(nc.psum_tensor("S2p", [128, 1024], F32))
        E["trp0"] = ctx.enter_context(nc.psum_tensor("trp0", [128, 1024], BF16))
        E["trp1"] = ctx.enter_context(nc.psum_tensor("trp1", [128, 1024], BF16))
        E["ovp"] = ctx.enter_context(nc.psum_tensor("ovp", [128, 512], F32))
        E["pob"] = ctx.enter_context(nc.psum_tensor("pob", [128, 1024], BF16))

        p = Prog()
        emit_all(p, E, dbg)
        p.resolve(nc, ctx)

        with nc.Block() as block:
            @block.tensor
            def _(e):
                p.emit("tensor", e)

            @block.scalar
            def _(e):
                p.emit("scalar", e)

            @block.vector
            def _(e):
                p.emit("vector", e)

            @block.gpsimd
            def _(e):
                p.emit("gpsimd", e)

            @block.sync
            def _(e):
                p.emit("sync", e)
    return nc


def emit_all(p, E, dbg):
    AB, AFr = E["AB"], E["AFr"]
    S1p, S2p, trp0, trp1, ovp, pob = E["S1p"], E["S2p"], E["trp0"], E["trp1"], E["ovp"], E["pob"]

    def ab(off, n, p0=0, np_=128):
        return AB[p0:p0 + np_, off:off + n]

    def af(off, n, p0=0, np_=128):
        return AFr[p0:p0 + np_, off:off + n]

    fIDENT, fONES, fRMSB, fLAM = 0, 128, 256, 512
    fBIG = 640
    bIDENT, bONES, bSEL = 0, 128, 256
    bBIG = bSEL + 2048

    ld_id = p.add("sync", lambda e: e.dma_start(out=af(fIDENT, 128), in_=E["identf"]), dma="win")
    ld_on = p.add("sync", lambda e: e.dma_start(out=af(fONES, 128, 0, 1), in_=E["onesf"]), dma="win")
    ld_idb = p.add("gpsimd", lambda e: e.dma_start(out=ab(bIDENT, 128), in_=E["identf"]), dma="wing")
    ld_sel = p.add("gpsimd", lambda e: e.dma_start(out=ab(bSEL, 2048, 0, NE), in_=E["self"]), dma="wing")
    ms_ones = p.add("gpsimd", lambda e: e.memset(ab(bONES, 128), 1.0))

    fSM = fBIG
    ld_rs = p.add("sync", lambda e: e.dma_start(out=af(fSM, DV, 0, 1), in_=E["rms_scale"]), dma="win")
    mm_rs = p.add("tensor", lambda e: e.matmul(ovp[:, 0:256], af(fONES, 128, 0, 1), af(fSM, DV, 0, 1), start=True, stop=True), waits=[ld_on, ld_rs])
    ev_rs = p.add("vector", lambda e: e.tensor_scalar(af(fRMSB, DV), ovp[:, 0:256], 1.0 - LAMBDA_INIT, None, ALU.mult), waits=[mm_rs])

    fL = fBIG + 256
    fLs = fL + 512
    ldl = []
    for i, t in enumerate(("lq1", "lk1", "lq2", "lk2")):
        ldl.append(p.add("sync", lambda e, t=t, i=i: e.dma_start(out=af(fL + 128 * i, 128, 0, 8), in_=E[t]), dma="win"))
    p.add("vector", lambda e: e.tensor_tensor(af(fL, 128, 0, 8), af(fL, 128, 0, 8), af(fL + 128, 128, 0, 8), ALU.mult), waits=ldl[:2])
    p.add("vector", lambda e: e.tensor_tensor(af(fL + 256, 128, 0, 8), af(fL + 256, 128, 0, 8), af(fL + 384, 128, 0, 8), ALU.mult), waits=ldl[2:])
    rs1 = p.add("vector", lambda e: e.tensor_reduce(af(fLs, 1, 0, 8), af(fL, 128, 0, 8), AX.X, ALU.add))
    rs2 = p.add("vector", lambda e: e.tensor_reduce(af(fLs + 1, 1, 0, 8), af(fL + 256, 128, 0, 8), AX.X, ALU.add))
    ex12 = p.add("scalar", lambda e: e.activation(af(fLs + 2, 2, 0, 8), af(fLs, 2, 0, 8), AF.Exp), waits=[rs1, rs2])
    p.add("vector", lambda e: e.tensor_tensor(af(fLs + 4, 1, 0, 8), af(fLs + 3, 1, 0, 8), af(fLs + 2, 1, 0, 8), ALU.subtract), waits=[ex12])
    nl2 = p.add("vector", lambda e: e.tensor_scalar(af(fLs + 5, 1, 0, 8), af(fLs + 4, 1, 0, 8), -LAMBDA_INIT, None, ALU.add))
    trl = p.add("tensor", lambda e: e.transpose(ovp[0:1, 256:264], af(fLs + 5, 1, 0, 8), af(fIDENT, 8, 0, 8)), waits=[nl2, ev_rs, ld_id])
    evl = p.add("vector", lambda e: e.tensor_copy(af(fLs + 6, 8, 0, 1), ovp[0:1, 256:264]), waits=[trl])
    mml = p.add("tensor", lambda e: e.matmul(ovp[:, 264:272], af(fONES, 128, 0, 1), af(fLs + 6, 8, 0, 1), start=True, stop=True), waits=[evl])
    nlam = p.add("vector", lambda e: e.tensor_copy(af(fLAM, 8), ovp[:, 264:272]), waits=[mml])

    bXTB = bBIG
    bW = bXTB + 32768
    bW2 = bW + 16384
    bSTG = bW2 + 16384
    assert bSTG + 4096 <= 73728

    xchunks = [p.add("gpsimd",
                     lambda e, c=c: e.dma_start(out=ab(bXTB + 4096 * c, 4096), in_=E["XT"][c * 128:(c + 1) * 128, :]),
                     dma="wing") for c in range(8)]
    ldwq = p.add("gpsimd", lambda e: e.dma_start(out=ab(bW, 16384), in_=E["WqT"]), dma="wing")
    ldwk = p.add("gpsimd", lambda e: e.dma_start(out=ab(bW2, 16384), in_=E["WkT"]), dma="wing")

    vh_r = E["VH"].rearrange("(t p) o -> p t o", p=128)

    def proj_fm(dst, woff, wait_w, tag, stg_prev=None, ev_seed=None):
        dmas = []
        prev_stg = stg_prev
        prev_ev = dict(ev_seed) if ev_seed else {}
        for m in range(16):
            evs = []
            for n in range(8):
                bank = (m * 8 + n) % 2
                psl = (S1p[:, 0:512], S1p[:, 512:1024])[bank]
                mms = []
                for c in range(8):
                    w = []
                    if c == 0:
                        w = [wait_w] + xchunks
                        if prev_ev.get(bank) is not None:
                            w.append(prev_ev[bank])
                    mms.append(p.add(
                        "tensor",
                        lambda e, m=m, n=n, c=c, psl=psl, woff=woff: e.matmul(
                            psl, ab(woff + 2048 * c + 128 * m, 128),
                            ab(bXTB + 4096 * c + 512 * n, 512),
                            start=(c == 0), stop=(c == 7)),
                        waits=w, tag=f"{tag}mm"))
                w2 = [mms[-1]]
                if n == 0:
                    if dmas:
                        w2.append(dmas[-1])
                    elif prev_stg is not None:
                        w2.append(prev_stg)
                ev = p.add("scalar", lambda e, n=n, psl=psl: e.copy(ab(bSTG + 512 * n, 512), psl), waits=w2)
                prev_ev[bank] = ev
                evs.append(ev)
            dmas.append(p.add(
                "sync",
                lambda e, m=m, dst=dst: e.dma_start(out=dst[m * 128:(m + 1) * 128, :], in_=ab(bSTG, 4096)),
                waits=evs, dma="out", tag=f"{tag}dma"))
        return dmas, prev_ev

    qdmas, qev = proj_fm(E["QTh"], bW, ldwq, "q")
    ldwv = p.add("gpsimd", lambda e: e.dma_start(out=ab(bW, 16384), in_=E["WvT"]), waits=[qdmas[-1]], dma="wing")
    kdmas, _ = proj_fm(E["KTh"], bW2, ldwk, "k", stg_prev=qdmas[-1], ev_seed=qev)

    vdmas = []
    prev_ev = {}
    for m in range(32):
        evs = []
        for gi in range(4):
            bank = (m * 4 + gi) % 2
            psl = (S2p[:, 0:512], S2p[:, 512:1024])[bank]
            mms = []
            for c in range(8):
                w = []
                if c == 0:
                    w = [ldwv]
                    if prev_ev.get(bank) is not None:
                        w.append(prev_ev[bank])
                mms.append(p.add(
                    "tensor",
                    lambda e, m=m, gi=gi, c=c, psl=psl: e.matmul(
                        psl, ab(bXTB + 4096 * c + 128 * m, 128),
                        ab(bW + 2048 * c + 512 * gi, 512),
                        start=(c == 0), stop=(c == 7)),
                    waits=w, tag="vmm"))
            w2 = [mms[-1]]
            if gi == 0:
                w2.append(vdmas[-1] if vdmas else kdmas[-1])
            ev = p.add("scalar", lambda e, gi=gi, psl=psl: e.copy(ab(bSTG + 512 * gi, 512), psl), waits=w2)
            prev_ev[bank] = ev
            evs.append(ev)
        vdmas.append(p.add(
            "sync",
            lambda e, m=m: e.dma_start(out=vh_r[:, m, :], in_=ab(bSTG, 2048)),
            waits=evs, dma="out"))

    if "stop_a0" in dbg:
        p.add("sync", lambda e: e.nop(), waits=qdmas + kdmas + vdmas)
        return

    bQ = bBIG
    bK = [bBIG + 8192, bBIG + 16384]
    bV = [bBIG + 24576, bBIG + 32768]
    bE1 = [bBIG + 40960, bBIG + 45056, bBIG + 49152]
    bE2 = [bBIG + 53248, bBIG + 57344]
    bPT = bBIG + 61440
    bOST = bPT + 4096
    bOTM = bOST + 1024
    assert bOTM + 512 <= 73728

    fD1 = [fBIG + 1024, fBIG + 1040]
    fD2 = [fBIG + 1028, fBIG + 1044]
    fDEN = [fBIG + 1056, fBIG + 1056 + 16]
    fON = [fBIG + 1100, fBIG + 1100 + 256]
    fSQ = fBIG + 1100 + 512

    qt_r = E["QTh"].rearrange("(c p) t -> p c t", p=128)
    kt_r = E["KTh"].rearrange("(c p) t -> p c t", p=128)

    def load_head(h, buf, waits):
        lq = p.add("sync", lambda e, h=h: e.dma_start(
            out=AB[:, bQ:bQ + 8192].rearrange("p (c t) -> p c t", c=2),
            in_=qt_r[:, 2 * h:2 * h + 2, :]), dma="win",
            waits=waits + (qdmas if h == 0 else []))
        lk = p.add("sync", lambda e, h=h, buf=buf: e.dma_start(
            out=AB[:, bK[buf]:bK[buf] + 8192].rearrange("p (c t) -> p c t", c=2),
            in_=kt_r[:, 2 * h:2 * h + 2, :]), dma="win",
            waits=(kdmas if h == 0 else []))
        lv = p.add("sync", lambda e, h=h, buf=buf: e.dma_start(
            out=AB[:, bV[buf]:bV[buf] + 8192].rearrange("p (t o) -> p t o", t=32),
            in_=vh_r[:, :, 256 * h:256 * h + 256]), dma="win",
            waits=(vdmas if h == 0 else []))
        return [lq, lk, lv]

    NU = H * 32
    st = {
        "loads": {0: load_head(0, 0, [])},
        "exp_last": [None, None],
        "comb": {}, "exps": {}, "dens": {}, "tr_last": {}, "pv": {},
        "epi": {}, "sq": {}, "otm": {}, "otr": {}, "ose": {}, "otdma": {},
        "head_last_exp": {},
        "trev_last": [None, None],
    }
    ot_dmas = []

    def emit_head_prefetch(u):
        h, qt = u // 32, u % 32
        if qt == 0 and h >= 1:
            lq = p.add("sync", lambda e, h=h: e.dma_start(
                out=AB[:, bQ:bQ + 8192].rearrange("p (c t) -> p c t", c=2),
                in_=qt_r[:, 2 * h:2 * h + 2, :]), dma="win",
                waits=[st["head_last_exp"][h - 1]])
            st["loads"][h].append(lq)
        if qt == 2 and h + 1 < H:
            w = []
            if h >= 1:
                w.append(st["pv"][(h - 1) * 32 + 31])
            lk = p.add("sync", lambda e, h1=h + 1, buf=1 - (h % 2): e.dma_start(
                out=AB[:, bK[buf]:bK[buf] + 8192].rearrange("p (c t) -> p c t", c=2),
                in_=kt_r[:, 2 * h1:2 * h1 + 2, :]), dma="win",
                waits=w)
            lv = p.add("sync", lambda e, h1=h + 1, buf=1 - (h % 2): e.dma_start(
                out=AB[:, bV[buf]:bV[buf] + 8192].rearrange("p (t o) -> p t o", t=32),
                in_=vh_r[:, :, 256 * h1:256 * h1 + 256]), dma="win")
            st["loads"][h + 1] = [lk, lv]


    def emit_S_span(u, j):
        h, qt = u // 32, u % 32
        buf = h % 2
        sl = u % 2
        loads = st["loads"][h]
        oQ, oK = bQ, bK[buf]
        exps = st["exps"].setdefault(u, {1: [], 2: []})
        for br, Sp, qoff, koff, bEo, fDx in (
                (1, S1p, oQ, oK, bE1[u % 3], fD1[sl]),
                (2, S2p, oQ + 4096, oK + 4096, bE2[sl], fD2[sl])):
            mms = []
            for half in range(2):
                kb = 2 * j + half
                w = []
                if kb == 0:
                    w = list(loads)
                    if st["exp_last"][br - 1] is not None:
                        w.append(st["exp_last"][br - 1])
                elif half == 0:
                    w.append(exps[br][j - 1])
                mms.append(p.add(
                    "tensor",
                    lambda e, qt=qt, kb=kb, Sp=Sp, qoff=qoff, koff=koff, half=half: e.matmul(
                        Sp[:, 512 * half:512 * half + 512],
                        ab(qoff + 128 * qt, 128), ab(koff + 512 * kb, 512),
                        start=True, stop=True),
                    waits=w, tag=f"s{br}u{u}"))
            ew = list(mms)
            if j == 0:
                if br == 1:
                    if (u - 3) in st["tr_last"]:
                        ew.append(st["tr_last"][u - 3])
                    if (u - 2) in st["dens"]:
                        ew.append(st["dens"][u - 2][0])
                if br == 2 and (u - 2) in st["comb"]:
                    ew.append(st["comb"][u - 2])
            exps[br].append(p.add(
                "scalar",
                lambda e, j=j, Sp=Sp, bEo=bEo, fDx=fDx: e.activation(
                    ab(bEo + 1024 * j, 1024), Sp[:, :], AF.Exp, scale=SCALE,
                    accum_out=af(fDx + j, 1)),
                waits=ew, tag=f"e{br}u{u}"))
            if j == 3:
                st["exp_last"][br - 1] = exps[br][-1]
                if qt == 31 and br == 2:
                    st["head_last_exp"][h] = exps[2][-1]

    def emit_TR(u, ktg_lo, ktg_hi):
        sl = u % 3
        c2 = st["comb"][u]
        for ktg in range(ktg_lo, ktg_hi + 1):
            dst = (trp0, trp1)[ktg % 2]
            trs = []
            for i in range(8):
                kt = 8 * ktg + i
                w = []
                if i == 0:
                    w = [c2, ld_idb]
                    if st["trev_last"][ktg % 2] is not None:
                        w.append(st["trev_last"][ktg % 2])
                trs.append(p.add(
                    "tensor",
                    lambda e, kt=kt, dst=dst, sl=sl: e.transpose(
                        dst[:, 128 * (kt % 8):128 * (kt % 8) + 128],
                        ab(bE1[sl] + 128 * kt, 128), ab(bIDENT, 128)),
                    waits=w, tag=f"tru{u}"))
            w2 = [trs[-1]]
            if (u - 1) in st["pv"]:
                w2.append(st["pv"][u - 1])
            tev = p.add(
                "vector",
                lambda e, ktg=ktg, dst=dst: e.tensor_copy(ab(bPT + 1024 * ktg, 1024), dst[:, :]),
                waits=w2, tag=f"trevu{u}")
            st["trev_last"][ktg % 2] = tev
            st.setdefault("trevs", {}).setdefault(u, []).append(tev)
            if ktg == 3:
                st["tr_last"][u] = trs[-1]

    def emit_PV(u, half):
        h = u // 32
        buf = h % 2
        sl = u % 2
        oV = bV[buf]
        trevs = st["trevs"][u]
        pvs = st.setdefault("pvs", {}).setdefault(u, [])
        for kt in range(16 * half, 16 * half + 16):
            w = []
            if kt % 8 == 0:
                w.append(trevs[kt // 8])
            if kt == 0:
                if (u - 1) in st["epi"]:
                    w.append(st["epi"][u - 1])
                if u == 0:
                    w += [nlam, ev_rs]
            pvs.append(p.add(
                "tensor",
                lambda e, kt=kt, sl=sl, oV=oV: e.matmul(
                    ovp[:, 256 * sl:256 * sl + 256],
                    ab(bPT + 128 * kt, 128), ab(oV + 256 * kt, 256),
                    start=(kt == 0), stop=(kt == 31)),
                waits=w, tag=f"pvu{u}"))
        if half == 1:
            st["pv"][u] = pvs[-1]

    def emit_epilogue(u):
        sl = u % 2
        fDx = fDEN[sl]
        fo = fON[sl]
        ms, r1 = st["dens"][u]
        ew = [st["pv"][u], r1]
        if (u - 2) in st["sq"]:
            ew.append(st["sq"][u - 2])
        epi = p.add("vector", lambda e, sl=sl, fo=fo, fDx=fDx: e.tensor_scalar(
            af(fo, 256), ovp[:, 256 * sl:256 * sl + 256], af(fDx + 4, 1), None, ALU.mult),
            waits=ew, tag=f"epiu{u}")
        st["epi"][u] = epi
        sqm = p.add("vector", lambda e, fo=fo: e.tensor_tensor(
            af(fSQ, 256), af(fo, 256), af(fo, 256), ALU.mult), tag=f"squ{u}")
        st["sq"][u] = sqm
        t2 = p.add("vector", lambda e, fDx=fDx: e.tensor_scalar(
            af(fSQ, 256), af(fSQ, 256), 1.0 / DV, EPS, ALU.mult, ALU.add,
            accum_out=af(fDx + 7, 1)), tag=f"t2u{u}")
        ln_ = p.add("scalar", lambda e, fDx=fDx: e.activation(af(fDx + 8, 1), af(fDx + 7, 1), AF.Ln), waits=[t2])
        rinv = p.add("scalar", lambda e, fDx=fDx: e.activation(af(fDx + 9, 1), af(fDx + 8, 1), AF.Exp, scale=-0.5), tag=f"rinvu{u}")
        p.add("vector", lambda e, fo=fo, fDx=fDx: e.tensor_scalar(af(fo, 256), af(fo, 256), af(fDx + 9, 1), None, ALU.mult), waits=[rinv])
        otm_w = [ev_rs]
        if (u - 2) in st["otr"]:
            otm_w.append(st["otr"][u - 2][-1])
        otm = p.add("vector", lambda e, sl=sl, fo=fo: e.tensor_tensor(
            ab(bOTM + 256 * sl, 256), af(fo, 256), af(fRMSB, 256), ALU.mult), waits=otm_w, tag=f"otmu{u}")
        st["otm"][u] = otm

    def emit_S_dve(u):
        h, qt = u // 32, u % 32
        sl = u % 2
        fDx = fDEN[sl]
        exps = st["exps"][u]
        p.add("vector", lambda e, sl=sl, fDx=fDx: e.tensor_reduce(
            af(fDx, 2), AFr[:, fD1[sl]:fD1[sl] + 8].rearrange("p (b x) -> p b x", b=2),
            AX.X, ALU.add), waits=exps[1] + exps[2])
        rd = p.add("vector", lambda e, fDx=fDx: e.reciprocal(af(fDx + 4, 2), af(fDx, 2)))
        p.add("vector", lambda e, fDx=fDx: e.tensor_tensor(af(fDx + 3, 1), af(fDx, 1), af(fDx + 5, 1), ALU.mult))
        ms = p.add("vector", lambda e, h=h, fDx=fDx: e.tensor_tensor(af(fDx + 6, 1), af(fDx + 3, 1), af(fLAM + h, 1), ALU.mult), waits=[nlam])
        st["dens"][u] = (ms, rd)
        c1 = p.add("vector", lambda e, sl=sl, fDx=fDx: e.tensor_scalar(ab(bE2[sl], 4096), ab(bE2[sl], 4096), af(fDx + 6, 1), None, ALU.mult), tag=f"c1u{u}")
        c2 = p.add("gpsimd", lambda e, sl=sl, s3=u % 3: e.tensor_tensor(ab(bE1[s3], 4096), ab(bE1[s3], 4096), ab(bE2[sl], 4096), ALU.add),
                   waits=[c1, exps[1][-1]], tag=f"c2u{u}")
        st["comb"][u] = c2

    def stage_OTR(u):
        h, qt = u // 32, u % 32
        sl = u % 2
        otrs = []
        for chv in range(2):
            w = [st["otm"][u]]
            if qt % 4 == 0 and chv == 0 and (u - 4) in st["ose"]:
                w += st["ose"][u - 4]
            otrs.append(p.add(
                "tensor",
                lambda e, qt=qt, chv=chv, sl=sl: e.transpose(
                    pob[:, 256 * (qt % 4) + 128 * chv:256 * (qt % 4) + 128 * chv + 128],
                    ab(bOTM + 256 * sl + 128 * chv, 128), ab(bIDENT, 128)),
                waits=w, tag=f"otru{u}"))
        st["otr"][u] = otrs
        if qt % 4 == 3:
            qb = u - 3
            grp = (qt - 3) // 4
            pob_v = pob[:, :].rearrange("p (q x) -> p q x", q=4)
            oses = []
            all_otrs = [o for i in range(4) for o in st["otr"][u - 3 + i]]
            for ch in range(2):
                w = list(all_otrs)
                prevd = st["otdma"].get((h, grp - 1), st["otdma"].get((h - 1, 7)))
                if prevd is not None:
                    w.append(prevd[ch])
                oses.append(p.add(
                    "vector",
                    lambda e, ch=ch: e.tensor_copy(
                        AB[:, bOST + 512 * ch:bOST + 512 * ch + 512].rearrange("p (q x) -> p q x", q=4),
                        pob_v[:, :, 128 * ch:128 * ch + 128]),
                    waits=w, tag=f"oseu{u}"))
            st["ose"][qb] = oses
            pair = []
            for ch in range(2):
                dma = p.add(
                    "sync",
                    lambda e, h=h, grp=grp, ch=ch: e.dma_start(
                        out=E["OT"][h * 256 + ch * 128:h * 256 + ch * 128 + 128, grp * 512:grp * 512 + 512],
                        in_=ab(bOST + 512 * ch, 512)),
                    waits=[oses[ch]], dma="out", tag=f"otdma{h}.{grp}")
                ot_dmas.append(dma)
                pair.append(dma)
            st["otdma"][(h, grp)] = pair

    nheads = H
    for d in dbg:
        if isinstance(d, str) and d.startswith("nh"):
            nheads = int(d[2:])
    NU = nheads * 32

    for u in range(NU + 4):
        cur = u < NU
        prv = 2 <= u <= NU + 1
        if cur:
            emit_head_prefetch(u)
            emit_S_span(u, 0)
        if 4 <= u <= NU + 3:
            stage_OTR(u - 4)
        if prv:
            emit_TR(u - 2, 0, 0)
        if cur:
            emit_S_span(u, 1)
        if prv:
            emit_TR(u - 2, 1, 2)
        if cur:
            emit_S_span(u, 2)
        if prv:
            emit_TR(u - 2, 3, 3)
            emit_PV(u - 2, 0)
        if cur:
            emit_S_span(u, 3)
        if prv:
            emit_PV(u - 2, 1)
            emit_epilogue(u - 2)
        if cur:
            emit_S_dve(u)

    if "stop_a1" in dbg:
        p.add("sync", lambda e: e.nop(), waits=ot_dmas)
        return

    bWO = bBIG
    bOB = [bWO + 16384, bWO + 24576]
    bRB = bWO + 32768
    bRSQ = bRB + 4096
    bX1B = bRSQ + 4096
    assert bX1B + 4096 <= 73728
    fXT = fBIG + 1024
    fR = fXT + 4096
    fX1 = fR + 4096
    fT = fX1 + 4096
    assert fT + 1536 <= 15488

    ldwo = p.add("gpsimd", lambda e: e.dma_start(out=ab(bWO, 16384), in_=E["WoT"]),
        dma="wing", waits=[st["head_last_exp"][H - 1], st["pv"][NU - 1]])
    ot_r = E["OT"].rearrange("(c p) t -> p c t", p=128)
    xt_r = E["XT"].rearrange("(c p) t -> p c t", p=128)
    x1f_r = E["X1F"].rearrange("(c p) t -> p c t", p=128)
    x1b_r = E["X1B"].rearrange("(c p) t -> p c t", p=128)

    prev = {}
    x1_dmas = []
    for n in range(8):
        bufn = n % 2
        lob_w = list(ot_dmas) if n == 0 else ([prev[("womm", n - 2)]] if ("womm", n - 2) in prev else [])
        lob = p.add("sync", lambda e, n=n, bufn=bufn: e.dma_start(
            out=AB[:, bOB[bufn]:bOB[bufn] + 8192].rearrange("p (c t) -> p c t", c=16),
            in_=ot_r[:, :, 512 * n:512 * n + 512]), dma="win", waits=lob_w)
        lxt = p.add("sync", lambda e, n=n: e.dma_start(
            out=AFr[:, fXT:fXT + 4096].rearrange("p (c t) -> p c t", c=8),
            in_=xt_r[:, :, 512 * n:512 * n + 512]), dma="win",
            waits=[prev[("rev_last", n - 1)]] if ("rev_last", n - 1) in prev else [])
        revs = []
        for m in range(8):
            psl = (S1p[:, 0:512], S1p[:, 512:1024])[m % 2]
            mms = []
            for ec in range(16):
                w = []
                if ec == 0:
                    w = [lob, ldwo]
                    if ("rev", n, m - 2) in prev:
                        w.append(prev[("rev", n, m - 2)])
                    if m < 2 and ("rev", n - 1, 6 + m) in prev:
                        w.append(prev[("rev", n - 1, 6 + m)])
                mms.append(p.add(
                    "tensor",
                    lambda e, m=m, ec=ec, psl=psl, bufn=bufn: e.matmul(
                        psl, ab(bWO + 1024 * ec + 128 * m, 128), ab(bOB[bufn] + 512 * ec, 512),
                        start=(ec == 0), stop=(ec == 15)),
                    waits=w, tag=f"wo{n}.{m}"))
            prev[("womm", n)] = mms[-1]
            rev = p.add("vector", lambda e, m=m, psl=psl: e.tensor_tensor(
                af(fR + 512 * m, 512), psl, af(fXT + 512 * m, 512), ALU.add),
                waits=[mms[-1], lxt], tag=f"rev{n}.{m}")
            prev[("rev", n, m)] = rev
            revs.append(rev)
        rbfs, rsqs = [], []
        for m in range(8):
            rbfs.append(p.add("scalar", lambda e, m=m: e.copy(ab(bRB + 512 * m, 512), af(fR + 512 * m, 512)), waits=[revs[m]]))
            rsqs.append(p.add("scalar", lambda e, m=m: e.activation(ab(bRSQ + 512 * m, 512), af(fR + 512 * m, 512), AF.Square), waits=[revs[m]]))
        summ, sqmm = [], []
        for c in range(8):
            w1 = [rbfs[c], ms_ones]
            w2 = [rsqs[c]]
            if c == 0 and ("stat", n - 1) in prev:
                w1.append(prev[("stat", n - 1)])
                w2.append(prev[("stat", n - 1)])
            summ.append(p.add("tensor", lambda e, c=c: e.matmul(
                S2p[:, 0:512], ab(bONES, 128), ab(bRB + 512 * c, 512), start=(c == 0), stop=(c == 7)), waits=w1))
            sqmm.append(p.add("tensor", lambda e, c=c: e.matmul(
                S2p[:, 512:1024], ab(bONES, 128), ab(bRSQ + 512 * c, 512), start=(c == 0), stop=(c == 7)), waits=w2))
        mb = p.add("vector", lambda e: e.tensor_scalar(af(fT, 512), S2p[:, 0:512], 1.0 / D, None, ALU.mult), waits=[summ[-1]])
        t = p.add("vector", lambda e: e.tensor_scalar(af(fT + 512, 512), S2p[:, 512:1024], 1.0 / D, EPS, ALU.mult, ALU.add), waits=[sqmm[-1]])
        prev[("stat", n)] = t
        p.add("vector", lambda e: e.tensor_tensor(af(fT + 1024, 512), af(fT, 512), af(fT, 512), ALU.mult))
        var = p.add("vector", lambda e: e.tensor_tensor(af(fT + 1024, 512), af(fT + 512, 512), af(fT + 1024, 512), ALU.subtract))
        lnv = p.add("scalar", lambda e: e.activation(af(fT + 512, 512), af(fT + 1024, 512), AF.Ln), waits=[var])
        rstd = p.add("scalar", lambda e: e.activation(af(fT + 1024, 512), af(fT + 512, 512), AF.Exp, scale=-0.5), waits=[lnv])
        x1f_ops, x1b_ops = [], []
        for m in range(8):
            wv = [rstd]
            if ("x1dma", n - 1) in prev and m == 0:
                wv.append(prev[("x1dma", n - 1)])
            p.add("vector", lambda e, m=m: e.tensor_tensor(af(fX1 + 512 * m, 512), af(fR + 512 * m, 512), af(fT, 512), ALU.subtract), waits=wv)
            x1c = p.add("vector", lambda e, m=m: e.tensor_tensor(af(fX1 + 512 * m, 512), af(fX1 + 512 * m, 512), af(fT + 1024, 512), ALU.mult))
            wb = [x1c]
            if m == 0 and ("x1dmab", n - 1) in prev:
                wb.append(prev[("x1dmab", n - 1)])
            x1b = p.add("scalar", lambda e, m=m: e.copy(ab(bX1B + 512 * m, 512), af(fX1 + 512 * m, 512)), waits=wb)
            x1f_ops.append(x1c)
            x1b_ops.append(x1b)
        rl = p.add("vector", lambda e: e.tensor_copy(af(fLAM, 1), af(fLAM, 1)))
        prev[("rev_last", n)] = rl
        d1 = p.add("sync", lambda e, n=n: e.dma_start(
            out=x1f_r[:, :, 512 * n:512 * n + 512],
            in_=AFr[:, fX1:fX1 + 4096].rearrange("p (c t) -> p c t", c=8)),
            waits=x1f_ops, dma="out")
        d2 = p.add("sync", lambda e, n=n: e.dma_start(
            out=x1b_r[:, :, 512 * n:512 * n + 512],
            in_=AB[:, bX1B:bX1B + 4096].rearrange("p (c t) -> p c t", c=8)),
            waits=x1b_ops, dma="out")
        prev[("x1dma", n)] = d1
        prev[("x1dmab", n)] = d2
        x1_dmas += [d1, d2]

    if "stop_b" in dbg:
        p.add("sync", lambda e: e.nop(), waits=x1_dmas)
        return

    bXB = bBIG
    bRACC = bXB + 8192
    bINT = bRACC + 8192
    bWG = [bINT + 8192, bINT + 20480, bINT + 32768]
    bG = bINT + 45056
    bCT = bG + 512
    bRW = bCT + 1024
    bCF = bRW + 128
    bLNB = bINT
    assert bCF + 16 <= 73728
    fCB = fBIG + 1024
    fLG = fCB + 1024
    fLGT = fLG + 1024
    fPRB = fLGT + 128
    fCFT = fPRB + 112
    fX1Q = fCFT + 16
    fR2 = fX1Q + 4096
    fT2 = fR2 + 4096
    assert fT2 + 1536 <= 15488

    ldrw = p.add("gpsimd", lambda e: e.dma_start(out=ab(bRW, 128), in_=E["rw"]), dma="wing",
        waits=[prev[("womm", 7)]])

    y_dmas = []
    cp = {}
    for Q in range(4):
        tq = 1024 * Q
        lxb = p.add("sync", lambda e, tq=tq: e.dma_start(
            out=AB[:, bXB:bXB + 8192].rearrange("p (c t) -> p c t", c=8),
            in_=x1b_r[:, :, tq:tq + 1024]), dma="win",
            waits=([prev[("x1dmab", n) ] for n in range(8)] if Q == 0 else [cp[("edn_last",)]]))
        lgevs = []
        for tt in range(2):
            mms = []
            for c in range(8):
                w = []
                if c == 0:
                    w = [lxb, ldrw]
                    if tt == 1:
                        w.append(lgevs[0])
                    elif ("ovp_free",) in cp:
                        w.append(cp[("ovp_free",)])
                    elif Q == 0:
                        w.append(st["epi"].get(NU - 1))
                mms.append(p.add("tensor", lambda e, tt=tt, c=c: e.matmul(
                    ovp[0:16, 0:512], ab(bRW + 16 * c, 16, 0, 128), ab(bXB + 1024 * c + 512 * tt, 512),
                    start=(c == 0), stop=(c == 7)), waits=w))
            lgevs.append(p.add("vector", lambda e, tt=tt: e.tensor_copy(af(fLG + 512 * tt, 512, 0, 16), ovp[0:16, 0:512]), waits=[mms[-1]]))
        trls = []
        for j in range(8):
            w = [lgevs[-1]] if j == 0 else []
            trls.append(p.add("tensor", lambda e, j=j: e.transpose(
                ovp[:, 16 * j:16 * j + 16], af(fLG + 128 * j, 128, 0, 16), af(fIDENT, 16, 0, 16)), waits=w))
        lgt = p.add("vector", lambda e: e.tensor_copy(af(fLGT, 128), ovp[:, 0:128]), waits=[trls[-1]])
        ctevs = []
        last_ovp_reader = lgt
        for j in range(8):
            m1r = p.add("vector", lambda e, j=j: e.reduce_max(af(fPRB, 1), af(fLGT + 16 * j, 16), AX.X))
            nm1 = p.add("vector", lambda e: e.tensor_scalar(af(fPRB + 5, 1), af(fPRB, 1), -1.0, None, ALU.mult))
            ex = p.add("scalar", lambda e, j=j: e.activation(af(fPRB + 16, 16), af(fLGT + 16 * j, 16), AF.Exp, bias=af(fPRB + 5, 1), accum_out=af(fPRB + 1, 1)), waits=[nm1])
            p.add("vector", lambda e: e.reciprocal(af(fPRB + 2, 1), af(fPRB + 1, 1)), waits=[ex])
            p.add("vector", lambda e: e.tensor_scalar(af(fPRB + 32, 16), af(fPRB + 16, 16), af(fPRB + 2, 1), None, ALU.mult))
            p.add("vector", lambda e: e.reduce_max(af(fPRB + 3, 1), af(fPRB + 32, 16), AX.X))
            p.add("vector", lambda e: e.tensor_scalar(af(fPRB + 48, 16), af(fPRB + 32, 16), af(fPRB + 3, 1), None, ALU.is_equal))
            p.add("vector", lambda e: e.tensor_tensor(af(fPRB + 64, 16), af(fPRB + 32, 16), af(fPRB + 48, 16), ALU.mult))
            p.add("vector", lambda e: e.tensor_tensor(af(fPRB + 64, 16), af(fPRB + 32, 16), af(fPRB + 64, 16), ALU.subtract))
            p.add("vector", lambda e: e.reduce_max(af(fPRB + 4, 1), af(fPRB + 64, 16), AX.X))
            p.add("vector", lambda e: e.tensor_scalar(af(fPRB + 48, 16), af(fPRB + 32, 16), af(fPRB + 4, 1), None, ALU.is_ge))
            cf = p.add("vector", lambda e: e.tensor_tensor(af(fCFT, 16), af(fPRB + 32, 16), af(fPRB + 48, 16), ALU.mult))
            cfb = p.add("scalar", lambda e: e.copy(ab(bCF, 16), af(fCFT, 16)), waits=[cf])
            trc = p.add("tensor", lambda e, j=j: e.transpose(trp0[0:16, 0:128], ab(bCF, 16), ab(bIDENT, 128)), waits=[cfb, (ctevs[-1] if ctevs else None)])
            ctevs.append(p.add("vector", lambda e, j=j: e.tensor_copy(ab(bCT + 128 * j, 128, 0, NE), trp0[0:16, 0:128]), waits=[trc]))
        lsg = p.add("gpsimd", lambda e: e.dma_start(out=ab(bWG[0], 8192), in_=E["sgw"]), dma="wing",
            waits=([ldrw, prev[("x1dmab", 7)], prev[("x1dma", 7)]] if Q == 0 else [cp[("edn_last",)]]))
        lsu = p.add("gpsimd", lambda e: e.dma_start(out=ab(bWG[1], 8192), in_=E["suw"]), dma="wing")
        last_sil = cp.get(("lnb_pe_done",))
        last_mi = None
        gu_done = []
        for fm in range(8):
            for tt in range(2):
                gm = []
                for c in range(8):
                    w = []
                    if c == 0:
                        w = [lsg, lxb]
                        if last_sil is not None:
                            w.append(last_sil)
                    gm.append(p.add("tensor", lambda e, fm=fm, tt=tt, c=c: e.matmul(
                        S1p[:, 0:512], ab(bWG[0] + 1024 * c + 128 * fm, 128), ab(bXB + 1024 * c + 512 * tt, 512),
                        start=(c == 0), stop=(c == 7)), waits=w))
                sil = p.add("scalar", lambda e: e.activation(ab(bG + 16, 496) if False else ab(bG, 512), S1p[:, 0:512], AF.Silu), waits=[gm[-1], (last_mi if last_mi is not None else None)], tag=f"sil{Q}.{fm}.{tt}")
                last_sil = sil
                um = []
                for c in range(8):
                    w = []
                    if c == 0:
                        w = [lsu]
                        if last_mi is not None:
                            w.append(last_mi)
                    um.append(p.add("tensor", lambda e, fm=fm, tt=tt, c=c: e.matmul(
                        S1p[:, 512:1024], ab(bWG[1] + 1024 * c + 128 * fm, 128), ab(bXB + 1024 * c + 512 * tt, 512),
                        start=(c == 0), stop=(c == 7)), waits=w))
                mi = p.add("vector", lambda e, fm=fm, tt=tt: e.tensor_tensor(
                    ab(bINT + 1024 * fm + 512 * tt, 512), ab(bG, 512), S1p[:, 512:1024], ALU.mult),
                    waits=[um[-1], sil], tag=f"smi{Q}.{fm}.{tt}")
                last_mi = mi
                gu_done.append(mi)
        lsd = p.add("gpsimd", lambda e: e.dma_start(out=ab(bWG[0], 8192), in_=E["sdw"]), dma="wing", waits=[gu_done[-1], last_sil])
        last_rc = [cp.get(("lnb_pe_done",)), cp.get(("lnb_pe_done",))]
        dn_ops = []
        for dm in range(8):
            for tt in range(2):
                bank = tt
                psl = (S2p[:, 0:512], S2p[:, 512:1024])[bank]
                mm = []
                for c in range(8):
                    w = []
                    if c == 0:
                        w = [lsd] + gu_done[tt::2]
                        if last_rc[bank] is not None:
                            w.append(last_rc[bank])
                    mm.append(p.add("tensor", lambda e, dm=dm, tt=tt, c=c, psl=psl: e.matmul(
                        psl, ab(bWG[0] + 1024 * c + 128 * dm, 128), ab(bINT + 1024 * c + 512 * tt, 512),
                        start=(c == 0), stop=(c == 7)), waits=w))
                rc = p.add("vector", lambda e, dm=dm, tt=tt, psl=psl: e.tensor_copy(
                    ab(bRACC + 1024 * dm + 512 * tt, 512), psl), waits=[mm[-1]], tag=f"sracc{Q}.{dm}.{tt}")
                last_rc[bank] = rc
                dn_ops.append(rc)
        for ei in range(NE):
            wbuf = ei % 3
            if ei >= 3:
                wld_w = [cp[("emi_last", ei - 3)], cp[("edn_mm_last", ei - 3)]]
            elif ei == 0:
                wld_w = [dn_ops[-1]]
            elif ei == 1:
                wld_w = [gu_done[-1], dn_ops[-1]]
            else:
                wld_w = [w for w in (cp.get(("emi_last", 14)), cp.get(("edn_mm_last", 14))) if w is not None]
            lg_ = p.add("gpsimd", lambda e, ei=ei, wbuf=wbuf: e.dma_start(
                out=ab(bWG[wbuf], 4096), in_=E["gw"][ei]), dma="wing", waits=wld_w)
            lu_ = p.add("gpsimd", lambda e, ei=ei, wbuf=wbuf: e.dma_start(
                out=ab(bWG[wbuf] + 4096, 4096), in_=E["uw"][ei]), dma="wing")
            ld_ = p.add("gpsimd", lambda e, ei=ei, wbuf=wbuf: e.dma_start(
                out=ab(bWG[wbuf] + 8192, 4096), in_=E["dw"][ei]), dma="wing")
            cbevs = []
            for tt in range(2):
                w = [ld_sel, ctevs[-1], last_ovp_reader]
                cbm = p.add("tensor", lambda e, ei=ei, tt=tt: e.matmul(
                    ovp[:, 0:512], ab(bSEL + 128 * ei, 128, 0, NE), ab(bCT + 512 * tt, 512, 0, NE),
                    start=True, stop=True), waits=w)
                cbev = p.add("vector", lambda e, tt=tt: e.tensor_copy(af(fCB + 512 * tt, 512), ovp[:, 0:512]), waits=[cbm])
                cbevs.append(cbev)
                last_ovp_reader = cbev
            emi = []
            last_esil = cp.get(("esil_last",))
            last_emi = cp.get(("emi_bank_last",))
            for fm in range(4):
                for tt in range(2):
                    gm = []
                    for c in range(8):
                        w = []
                        if c == 0:
                            w = [lg_]
                            if ei == 0 and fm == 0 and tt == 0:
                                w.append(dn_ops[-1])
                                w.append(gu_done[-1])
                            if last_esil is not None:
                                w.append(last_esil)
                        gm.append(p.add("tensor", lambda e, fm=fm, tt=tt, c=c, wbuf=wbuf: e.matmul(
                            S1p[:, 0:512], ab(bWG[wbuf] + 512 * c + 128 * fm, 128), ab(bXB + 1024 * c + 512 * tt, 512),
                            start=(c == 0), stop=(c == 7)), waits=w))
                    sil = p.add("scalar", lambda e: e.activation(ab(bG, 512), S1p[:, 0:512], AF.Silu),
                                waits=[gm[-1], (last_emi if last_emi is not None else None)])
                    last_esil = sil
                    um = []
                    for c in range(8):
                        w = []
                        if c == 0:
                            w = [lu_]
                            if last_emi is not None:
                                w.append(last_emi)
                        um.append(p.add("tensor", lambda e, fm=fm, tt=tt, c=c, wbuf=wbuf: e.matmul(
                            S1p[:, 512:1024], ab(bWG[wbuf] + 4096 + 512 * c + 128 * fm, 128), ab(bXB + 1024 * c + 512 * tt, 512),
                            start=(c == 0), stop=(c == 7)), waits=w))
                    mi1 = p.add("vector", lambda e, fm=fm, tt=tt: e.tensor_tensor(
                        ab(bINT + 1024 * fm + 512 * tt, 512), ab(bG, 512), S1p[:, 512:1024], ALU.mult),
                        waits=[um[-1], sil])
                    mi2 = p.add("vector", lambda e, fm=fm, tt=tt: e.tensor_tensor(
                        ab(bINT + 1024 * fm + 512 * tt, 512), ab(bINT + 1024 * fm + 512 * tt, 512), af(fCB + 512 * tt, 512), ALU.mult),
                        waits=[cbevs[tt]], tag=f"emi{Q}.{ei}.{fm}.{tt}")
                    last_emi = mi1
                    emi.append(mi2)
            cp[("esil_last",)] = last_esil
            cp[("emi_bank_last",)] = last_emi
            cp[("emi_last", ei)] = emi[-1]
            edn = []
            for dm in range(8):
                for tt in range(2):
                    bank = tt
                    psl = (S2p[:, 0:512], S2p[:, 512:1024])[bank]
                    mm = []
                    for c in range(4):
                        w = []
                        if c == 0:
                            w = [ld_] + emi[tt::2]
                            if last_rc[bank] is not None:
                                w.append(last_rc[bank])
                        mm.append(p.add("tensor", lambda e, dm=dm, tt=tt, c=c, psl=psl, wbuf=wbuf: e.matmul(
                            psl, ab(bWG[wbuf] + 8192 + 1024 * c + 128 * dm, 128), ab(bINT + 1024 * c + 512 * tt, 512),
                            start=(c == 0), stop=(c == 3)), waits=w))
                    rc = p.add("vector", lambda e, dm=dm, tt=tt, psl=psl: e.tensor_tensor(
                        ab(bRACC + 1024 * dm + 512 * tt, 512), ab(bRACC + 1024 * dm + 512 * tt, 512), psl, ALU.add),
                        waits=[mm[-1]], tag=f"eracc{Q}.{ei}.{dm}.{tt}")
                    last_rc[bank] = rc
                    edn.append(rc)
            cp[("edn_mm_last", ei)] = mm[-1]
            cp[("edn_last",)] = edn[-1]
        cp[("ovp_free",)] = last_ovp_reader
        lnb_pe = None
        for tt in range(2):
            if Q == 0 and tt == 0:
                lx1_w = [prev[("x1dma", n)] for n in range(8)]
            elif tt == 1:
                lx1_w = [cp[("ydma", Q, 0)]]
            else:
                lx1_w = [cp[("ydma", Q - 1, 1)]]
            lx1 = p.add("sync", lambda e, tq=tq, tt=tt: e.dma_start(
                out=AFr[:, fX1Q:fX1Q + 4096].rearrange("p (c t) -> p c t", c=8),
                in_=x1f_r[:, :, tq + 512 * tt:tq + 512 * tt + 512]), dma="win", waits=lx1_w)
            r2s = []
            for m in range(8):
                r2s.append(p.add("vector", lambda e, m=m, tt=tt: e.tensor_tensor(
                    af(fR2 + 512 * m, 512), af(fX1Q + 512 * m, 512), ab(bRACC + 1024 * m + 512 * tt, 512), ALU.add),
                    waits=[lx1, cp[("edn_last",)]], tag=f"r2.{Q}.{tt}.{m}"))
            rbfs, rsqs = [], []
            for m in range(8):
                w = [r2s[m]]
                if tt == 0 and m == 0:
                    w.append(cp[("emi_last", NE - 1)])
                rbfs.append(p.add("scalar", lambda e, m=m: e.copy(ab(bLNB + 512 * m, 512), af(fR2 + 512 * m, 512)), waits=w))
                rsqs.append(p.add("scalar", lambda e, m=m: e.activation(ab(bLNB + 4096 + 512 * m, 512), af(fR2 + 512 * m, 512), AF.Square), waits=[r2s[m]]))
            summ, sqmm = [], []
            for c in range(8):
                w1 = [rbfs[c], ms_ones]
                w2 = [rsqs[c]]
                if c == 0:
                    w1.append(cp[("edn_mm_last", NE - 1)])
                    if ("lnstat", Q, tt - 1) in cp:
                        w1.append(cp[("lnstat", Q, tt - 1)])
                        w2.append(cp[("lnstat", Q, tt - 1)])
                    elif ("lnstat", Q - 1, 1) in cp:
                        w1.append(cp[("lnstat", Q - 1, 1)])
                        w2.append(cp[("lnstat", Q - 1, 1)])
                summ.append(p.add("tensor", lambda e, c=c: e.matmul(
                    S1p[:, 0:512], ab(bONES, 128), ab(bLNB + 512 * c, 512), start=(c == 0), stop=(c == 7)), waits=w1))
                sqmm.append(p.add("tensor", lambda e, c=c: e.matmul(
                    S1p[:, 512:1024], ab(bONES, 128), ab(bLNB + 4096 + 512 * c, 512), start=(c == 0), stop=(c == 7)), waits=w2))
            mb = p.add("vector", lambda e: e.tensor_scalar(af(fT2, 512), S1p[:, 0:512], 1.0 / D, None, ALU.mult), waits=[summ[-1]])
            t = p.add("vector", lambda e: e.tensor_scalar(af(fT2 + 512, 512), S1p[:, 512:1024], 1.0 / D, EPS, ALU.mult, ALU.add), waits=[sqmm[-1]])
            cp[("lnstat", Q, tt)] = t
            p.add("vector", lambda e: e.tensor_tensor(af(fT2 + 1024, 512), af(fT2, 512), af(fT2, 512), ALU.mult))
            var = p.add("vector", lambda e: e.tensor_tensor(af(fT2 + 1024, 512), af(fT2 + 512, 512), af(fT2 + 1024, 512), ALU.subtract))
            lnv = p.add("scalar", lambda e: e.activation(af(fT2 + 512, 512), af(fT2 + 1024, 512), AF.Ln), waits=[var])
            rstd = p.add("scalar", lambda e: e.activation(af(fT2 + 1024, 512), af(fT2 + 512, 512), AF.Exp, scale=-0.5), waits=[lnv])
            yevs = []
            for m in range(8):
                p.add("vector", lambda e, m=m: e.tensor_tensor(af(fX1Q + 512 * m, 512), af(fR2 + 512 * m, 512), af(fT2, 512), ALU.subtract), waits=[rstd])
                yevs.append(p.add("vector", lambda e, m=m: e.tensor_tensor(af(fX1Q + 512 * m, 512), af(fX1Q + 512 * m, 512), af(fT2 + 1024, 512), ALU.mult)))
            yd = p.add("sync", lambda e, tq=tq, tt=tt: e.dma_start(
                out=E["YT"].rearrange("(c p) t -> p c t", p=128)[:, :, tq + 512 * tt:tq + 512 * tt + 512],
                in_=AFr[:, fX1Q:fX1Q + 4096].rearrange("p (c t) -> p c t", c=8)),
                waits=yevs, dma="out")
            cp[("ydma", Q, tt)] = yd
            y_dmas.append(yd)
        cp[("lnb_pe_done",)] = cp[("lnstat", Q, 1)]

    p.add("sync", lambda e: e.nop(), waits=y_dmas)


def make_sel():
    s = np.zeros((NE, NE * 128), dtype=np.float32)
    for e in range(NE):
        s[e, e * 128:(e + 1) * 128] = 1.0
    return s


def swz(m):
    m = np.asarray(m, dtype=np.float32)
    c = m.shape[0] // 128
    return np.ascontiguousarray(m.reshape(c, 128, m.shape[1]).transpose(1, 0, 2).reshape(128, -1))


def make_inputs_for_core(inputs, b):
    f32 = lambda x: np.ascontiguousarray(np.asarray(x, dtype=np.float32))
    d = {}
    d["XT"] = f32(np.asarray(inputs["L0_attn_X"][b]).T)
    d["WqT"] = swz(np.asarray(inputs["L0_attn_Wq"]).T)
    d["WkT"] = swz(np.asarray(inputs["L0_attn_Wk"]).T)
    d["WvT"] = swz(np.asarray(inputs["L0_attn_Wv"]).T)
    d["WoT"] = swz(np.asarray(inputs["L0_attn_Wo"]).T)
    d["lq1"] = f32(inputs["L0_attn_lambda_q1"])
    d["lk1"] = f32(inputs["L0_attn_lambda_k1"])
    d["lq2"] = f32(inputs["L0_attn_lambda_q2"])
    d["lk2"] = f32(inputs["L0_attn_lambda_k2"])
    d["rms_scale"] = f32(np.asarray(inputs["L0_attn_rms_scale"])[None, :])
    d["rw"] = swz(np.asarray(inputs["L0_ffn_router_w"]))
    d["gw"] = np.stack([swz(np.asarray(inputs["L0_ffn_gate_w"])[e]) for e in range(NE)])
    d["uw"] = np.stack([swz(np.asarray(inputs["L0_ffn_up_w"])[e]) for e in range(NE)])
    d["dw"] = np.stack([swz(np.asarray(inputs["L0_ffn_down_w"])[e]) for e in range(NE)])
    d["sgw"] = swz(np.asarray(inputs["L0_ffn_shared_gate_w"]))
    d["suw"] = swz(np.asarray(inputs["L0_ffn_shared_up_w"]))
    d["sdw"] = swz(np.asarray(inputs["L0_ffn_shared_down_w"]))
    d["identf"] = np.eye(128, dtype=np.float32)
    d["onesf"] = np.ones((1, 128), dtype=np.float32)
    d["self"] = make_sel()
    return d


def kernel(**inputs):
    nc = build_program()
    in_maps = [make_inputs_for_core(inputs, b) for b in range(8)]
    res = run_bass_kernel_spmd(nc, in_maps, core_ids=list(range(8)))
    out = np.stack([np.asarray(r["YT"]).T for r in res.results], axis=0)
    return np.ascontiguousarray(out).astype(np.float32)
